# version 17
# baseline (speedup 1.0000x reference)
import contextlib
import numpy as np
import ml_dtypes
import concourse.bass as bass
import concourse.mybir as mybir
from concourse.bass_utils import run_bass_kernel_spmd

F32 = mybir.dt.float32
BF16 = mybir.dt.bfloat16
AF = mybir.ActivationFunctionType
ALU = mybir.AluOpType
AX = mybir.AxisListType

P = 128
MAGIC = 12582912.0
TWO_PI = 6.2831854820251465
INV_2PI = 0.15915493667125702
PI_LO = 3.1415925025939941
PI_HALF = 1.5707963267948966


class Cfg:
    def __init__(self, KT=8, NCO=32, NE=8, NLT=16, ACH=512, FCH=512, W1CO=512,
                 RING=2):
        self.KT = KT
        self.DM = KT * P
        self.NCO = NCO
        self.DFF = NCO * P
        self.NE = NE
        self.NLT = NLT
        self.TOWN = NLT * P
        self.ACH = ACH
        self.NACH = self.TOWN // ACH
        self.ATT = ACH // P
        self.FCH = FCH
        self.NTT = FCH // P
        self.NFCH = self.TOWN // FCH
        self.W1CO = W1CO
        self.NW1C = W1CO // P
        self.NCOB = self.DFF // W1CO
        self.RING = RING


class Trk:
    def __init__(self, sem, step=1):
        self.sem, self.n, self.step = sem, 0, step

    def __init_seen(self):
        if not hasattr(self, "_seen"):
            self._seen = set()

    def inc(self, instr):
        self.__init_seen()
        if id(instr) not in self._seen:
            self._seen.add(id(instr))
            self._keep = getattr(self, "_keep", [])
            self._keep.append(instr)
            instr.then_inc(self.sem, self.step)
            self.n += self.step
        return (self, self.n)

    @property
    def tok(self):
        return (self, self.n)


class Waiter:
    def __init__(self):
        self.wm = {}

    def need(self, eng, tok):
        if tok is None:
            return
        if isinstance(tok[0], (tuple, list)):
            for t in tok:
                self.need(eng, t)
            return
        trk, val = tok
        if val is None or val <= 0:
            return
        key = id(trk)
        if self.wm.get(key, 0) < val:
            eng.wait_ge(trk.sem, val)
            self.wm[key] = val

    def bump(self, tok):
        trk, val = tok
        key = id(trk)
        self.wm[key] = max(self.wm.get(key, 0), val)


_AE_OPS = {
    "memset", "memzero", "tensor_copy", "tensor_tensor", "tensor_add",
    "tensor_sub", "tensor_mul", "tensor_max", "tensor_scalar",
    "tensor_single_scalar", "tensor_scalar_add", "tensor_scalar_sub",
    "tensor_scalar_mul", "tensor_scalar_max", "tensor_scalar_min",
    "tensor_relu", "tensor_reduce", "reduce_max", "reduce_sum",
    "scalar_tensor_tensor", "tensor_tensor_scan", "bn_stats", "bn_aggr",
    "reciprocal", "select", "transpose", "activation", "copy", "mul", "add",
    "sqrt", "square", "sign",
}


class AutoEdge:

    def __init__(self, eng, trk, waiter):
        self._e, self._t, self._w = eng, trk, waiter

    def __getattr__(self, nm):
        fn = getattr(self._e, nm)
        if nm in _AE_OPS:
            eng, trk, w = self._e, self._t, self._w

            def wrapped(*a, **k):
                w.need(eng, trk.tok)
                ins = fn(*a, **k)
                trk.inc(ins)
                return ins

            return wrapped
        return fn


def build_program(cfg: Cfg, phases=5):
    nc = bass.Bass("TRN2", target_bir_lowering=False)
    KT, DM, NCO, DFF, NE = cfg.KT, cfg.DM, cfg.NCO, cfg.DFF, cfg.NE
    NLT, TOWN, ACH, NACH, ATT = cfg.NLT, cfg.TOWN, cfg.ACH, cfg.NACH, cfg.ATT
    FCH, NTT, NFCH = cfg.FCH, cfg.NTT, cfg.NFCH
    W1CO, NW1C, NCOB, RING = cfg.W1CO, cfg.NW1C, cfg.NCOB, cfg.RING
    NHPS = 2
    assert cfg.NW1C % NHPS == 0 or cfg.NW1C <= NHPS
    assert cfg.ACH == min(512, cfg.DM)
    MMW = min(512, DM)
    NH = DM // MMW
    KCG = min(4, NCO)
    assert NCO % KCG == 0 and KCG % NHPS == 0
    NKCG = NCO // KCG
    W2SLOTS = min(8, max(1, NKCG * NH))
    assert cfg.NTT + NHPS + 2 <= 8
    AW = min(512, ACH)
    ANH = ACH // AW
    SCRW = 2 * TOWN

    xown = nc.dram_tensor("xown", [TOWN, DM], F32, kind="ExternalInput")
    xfh = nc.dram_tensor("xfh", [TOWN, DM], F32, kind="ExternalInput")
    wk = nc.dram_tensor("wk", [DM, DM], F32, kind="ExternalInput")
    wv = nc.dram_tensor("wv", [DM, DM], F32, kind="ExternalInput")
    wo = nc.dram_tensor("wo", [DM, DM], F32, kind="ExternalInput")
    gate_w = nc.dram_tensor("gate_w", [DM, NE], F32, kind="ExternalInput")
    w1 = nc.dram_tensor("w1", [NE, DM, DFF], F32, kind="ExternalInput")
    w2 = nc.dram_tensor("w2", [NE, DFF, DM], F32, kind="ExternalInput")
    b1t = nc.dram_tensor("b1t", [P, NE * NCO], F32, kind="ExternalInput")
    b2 = nc.dram_tensor("b2", [NE, DM], F32, kind="ExternalInput")
    fcol_d = nc.dram_tensor("fcol", [P, KT], F32, kind="ExternalInput")
    gpos_d = nc.dram_tensor("gpos", [P, KT], F32, kind="ExternalInput")
    gas_d = nc.dram_tensor("gassoc", [P, KT], F32, kind="ExternalInput")
    trow_own = nc.dram_tensor("trow_own", [1, TOWN], F32, kind="ExternalInput")
    trow_fh = nc.dram_tensor("trow_fh", [1, TOWN], F32, kind="ExternalInput")
    rsrow = nc.dram_tensor("rsrow", [1, TOWN], BF16, kind="ExternalInput")
    ident_d = nc.dram_tensor("ident", [P, P], F32, kind="ExternalInput")
    bmask_d = nc.dram_tensor("bmask", [P, 1], F32, kind="ExternalInput")
    out_d = nc.dram_tensor("out", [TOWN, DM], F32, kind="ExternalOutput")
    xnscr = nc.dram_tensor("xnscr", [KT, P, SCRW], BF16)
    wkc = nc.dram_tensor("wkc", [KT * P, KT * P], BF16)
    wvc = nc.dram_tensor("wvc", [KT * P, KT * P], BF16)
    woc = nc.dram_tensor("woc", [NH * P, KT * MMW], BF16)
    w1c = nc.dram_tensor("w1c", [NE * NCOB * P, KT * W1CO], BF16)
    w2c = nc.dram_tensor("w2c", [NE * NH * NKCG * P, KCG * MMW], BF16)

    es = contextlib.ExitStack()
    sbt = lambda nm, shp, dt=F32: es.enter_context(nc.sbuf_tensor(nm, shp, dt))

    accf = sbt("accf", [P, NLT * DM], BF16)
    se_tm = sbt("se_tm", [P, NLT * NE], F32)
    se_em = sbt("se_em", [NE, TOWN], BF16)
    carries = sbt("carries", [P, KT * 4], F32)
    bndc = sbt("bndc", [P, KT * 2], F32)
    consts = sbt("consts", [P, 4], F32)
    fcol = sbt("fcol_s", [P, KT], F32)
    f2col = sbt("f2col_s", [P, KT], F32)
    gposc = sbt("gpos_s", [P, KT], F32)
    gac = sbt("gas_s", [P, KT], F32)
    b1sb = sbt("b1_s", [P, NE * NCO], F32)
    b2sb = sbt("b2_s", [NE, DM], BF16)
    gwsb = sbt("gw_s", [P, KT * NE], F32)
    identsb = sbt("ident_s", [P, P], F32)
    bmaskc = sbt("bmask_s", [P, 1], F32)
    w1ring = sbt("w1ring", [P, RING * KT * W1CO], BF16)

    sem_t = es.enter_context(nc.semaphore("sem_t"))
    sem_v = es.enter_context(nc.semaphore("sem_v"))
    sem_s = es.enter_context(nc.semaphore("sem_s"))
    T = Trk(sem_t); V = Trk(sem_v); S = Trk(sem_s)
    D = {}
    for nm in ("ysetup", "yx0", "yx1", "yscr0", "yscr1", "yxna", "ytr",
               "yxnf0", "yxnf1", "yout0", "yout1",
               "gsetup", "gwk", "gwv", "gwo",
               "gp0", "gp1", "gp2", "gp3", "gp4", "gp5", "gp6", "gp7",
               "gcw0", "gcw1", "gcw2", "gcw3",
               "yw10", "yw11", "yw12", "gcast", "gcastw"):
        D[nm] = Trk(es.enter_context(nc.semaphore("d_" + nm)), 16)
    wt, wvv, ws, wg, wy = Waiter(), Waiter(), Waiter(), Waiter(), Waiter()
    AE_V = lambda eng: AutoEdge(eng, V, wvv)
    AE_S = lambda eng: AutoEdge(eng, S, ws)

    block = es.enter_context(nc.Block())

    with es:
        @block.sync
        def _(y):
            D["ysetup"].inc(y.dma_start(fcol[:], fcol_d[:, :]))
            D["ysetup"].inc(y.dma_start(gposc[:], gpos_d[:, :]))
            D["ysetup"].inc(y.dma_start(gac[:], gas_d[:, :]))
            D["ysetup"].inc(y.dma_start(b1sb[:], b1t[:, :]))
            D["ysetup"].inc(y.dma_start(identsb[:], ident_d[:, :]))
            D["ysetup"].inc(y.dma_start(bmaskc[:], bmask_d[:, :]))
            D["ysetup"].inc(y.dma_start(
                bass.AP(gwsb, 0, [[KT * NE, P], [NE, KT], [1, NE]]),
                bass.AP(gate_w, 0, [[NE, P], [NE * P, KT], [1, NE]])))
        tok_setup_y = D["ysetup"].tok

        @block.gpsimd
        def _(g):
            D["gsetup"].inc(g.dma_start(b2sb[:], b2[:, :]))
        tok_b2 = D["gsetup"].tok

        tok_cast_kv = None
        tok_cast_w1 = [None] * NE
        tok_cast_w2 = [None] * NE

        cast_pieces = []
        for _j in range(KT):
            cast_pieces.append(("kv", _j))
        for _hh in range(NH):
            cast_pieces.append(("wo", _hh))
        for _e in range(NE):
            for _cob in range(NCOB):
                cast_pieces.append(("w1", _e, _cob))
            for _hh in range(NH):
                for _kcg in range(NKCG):
                    cast_pieces.append(("w2", _e, _hh, _kcg))
        cast_idx = [0]
        cast_rr = [0]

        def all_cast_toks():
            return tuple(D["gcw%d" % k].tok for k in range(4))

        def emit_cast_pieces(k):
            if cast_idx[0] >= len(cast_pieces):
                return
            @block.gpsimd
            def _(g):
                nonlocal tok_cast_kv
                for _ in range(k):
                    if cast_idx[0] >= len(cast_pieces):
                        return
                    pc = cast_pieces[cast_idx[0]]
                    cast_idx[0] += 1
                    if pc[0] in ("kv", "wo"):
                        wg.need(g, D["gcast"].tok)
                        sem = D["gcast"]
                    else:
                        sem = D["gcw%d" % (cast_rr[0] % 4)]
                        cast_rr[0] += 1
                        wg.need(g, sem.tok)
                    if pc[0] == "kv":
                        j = pc[1]
                        D["gcast"].inc(g.dma_start(
                            bass.AP(wkc, j * P * KT * P,
                                    [[KT * P, P], [P, KT], [1, P]]),
                            bass.AP(wk, j * P, [[DM, P], [DM * P, KT], [1, P]])))
                        wg.need(g, D["gcast"].tok)
                        D["gcast"].inc(g.dma_start(
                            bass.AP(wvc, j * P * KT * P,
                                    [[KT * P, P], [P, KT], [1, P]]),
                            bass.AP(wv, j * P, [[DM, P], [DM * P, KT], [1, P]])))
                    elif pc[0] == "wo":
                        hh = pc[1]
                        tok = D["gcast"].inc(g.dma_start(
                            bass.AP(woc, hh * P * KT * MMW,
                                    [[KT * MMW, P], [MMW, KT], [1, MMW]]),
                            bass.AP(wo, hh * MMW,
                                    [[DM, P], [DM * P, KT], [1, MMW]])))
                        if hh == NH - 1:
                            tok_cast_kv = tok
                    elif pc[0] == "w1":
                        _e, cob = pc[1], pc[2]
                        sem.inc(g.dma_start(
                            bass.AP(w1c, (_e * NCOB + cob) * P * KT * W1CO,
                                    [[KT * W1CO, P], [W1CO, KT], [1, W1CO]]),
                            bass.AP(w1, _e * DM * DFF + cob * W1CO,
                                    [[DFF, P], [DFF * P, KT], [1, W1CO]])))
                        if cob == NCOB - 1:
                            tok_cast_w1[_e] = all_cast_toks()
                    else:
                        _e, hh, kcg = pc[1], pc[2], pc[3]
                        sem.inc(g.dma_start(
                            bass.AP(w2c,
                                    ((_e * NH + hh) * NKCG + kcg) * P * KCG * MMW,
                                    [[KCG * MMW, P], [MMW, KCG], [1, MMW]]),
                            bass.AP(w2, _e * DFF * DM + kcg * KCG * P * DM
                                    + hh * MMW,
                                    [[DM, P], [DM * P, KCG], [1, MMW]])))
                        if hh == NH - 1 and kcg == NKCG - 1:
                            tok_cast_w2[_e] = all_cast_toks()

        def finish_casts():
            emit_cast_pieces(len(cast_pieces))

        def force_casts_until(fn):
            while fn() is None and cast_idx[0] < len(cast_pieces):
                emit_cast_pieces(1)

        @block.vector
        def _(v):
            v = AE_V(v)
            v.memset(consts[:, 0:1], MAGIC)
            v.memset(consts[:, 1:2], PI_HALF)
            v.memset(consts[:, 2:3], 0.0)
            v.memset(consts[:, 3:4], 1e-5)
            v.memset(carries[:], 0.0)
            v.memset(bndc[:], 0.0)
            wvv.need(v, tok_setup_y)
            V.inc(v.tensor_scalar_mul(f2col[:], fcol[:], INV_2PI))
        tok_setup_v = V.tok

        with contextlib.ExitStack() as ph:
            psb = lambda nm, shp, dt=F32: ph.enter_context(nc.sbuf_tensor(nm, shp, dt))
            xbuf = psb("ln_x", [P, 4 * DM])
            xnf = [psb("ln_xn%d" % k_, [P, DM]) for k_ in range(2)]
            stats = [psb("ln_st%d" % k_, [P, 12]) for k_ in range(2)]
            mv = [psb("ln_mv%d" % k_, [P, 8]) for k_ in range(2)]
            xnf32b = [psb("ln_f32%d" % k_, [P, KT * P]) for k_ in range(2)]
            xnstg = psb("ln_stg", [P, 2 * KT * P], BF16)
            expb = psb("ln_exp", [P, NE])
            trps = [ph.enter_context(nc.psum_tensor("ln_tr%d" % k_, [P, KT * P], F32))
                    for k_ in range(2)]
            lps = ph.enter_context(nc.psum_tensor("ln_lg", [P, NE], F32))
            seps = ph.enter_context(nc.psum_tensor("ln_se", [NE, P], F32))

            NT2 = 2 * NLT if phases >= 1 else 0
            tok_x = [None] * NT2
            tok_v_stats = [None] * NT2
            tok_s_r = [None] * NT2
            tok_v_xn = [None] * NT2
            tok_t_tr = [None] * NT2
            tok_v_evac = [None] * NT2
            tok_s_stage = [None] * NT2
            tok_y_scr = [None] * NT2
            tok_t_rt = [None] * NT2
            tok_t_se = [None] * NT2
            tok_v_se = [None] * NT2
            tok_s_seem = [None] * NT2

            for i in range(NT2):
                fh = i < NLT
                ri = i if fh else i - NLT
                xsrc = xfh if fh else xown
                scol = (0 if fh else TOWN) + ri * P
                sl4 = i % 4
                pr = i % 2
                prev_own = i - 1 if i - 1 >= NLT else None
                prev_own2 = i - 2 if i - 2 >= NLT else None

                @block.sync
                def _(y, i=i, ri=ri, xsrc=xsrc, sl4=sl4):
                    if i >= 4:
                        wy.need(y, tok_v_xn[i - 4])
                    tok_x[i] = D["yx%d" % (sl4 % 2)].inc(y.dma_start(
                        xbuf[:, sl4 * DM:(sl4 + 1) * DM],
                        xsrc[ri * P:(ri + 1) * P, :]))

                @block.vector
                def _(v, i=i, sl4=sl4, pr=pr):
                    v = AE_V(v)
                    wvv.need(v, tok_x[i])
                    if i >= 2:
                        wvv.need(v, tok_s_r[i - 2])
                    v.bn_stats(stats[pr][:, 0:6], xbuf[:, sl4 * DM: sl4 * DM + DM // 2])
                    v.bn_stats(stats[pr][:, 6:12],
                               xbuf[:, sl4 * DM + DM // 2:(sl4 + 1) * DM])
                    v.bn_aggr(mv[pr][:, 0:2], stats[pr][:, 0:12])
                    v.tensor_scalar_add(mv[pr][:, 2:3], mv[pr][:, 1:2], consts[:, 3:4])
                    tok_v_stats[i] = V.tok

                @block.scalar
                def _(s, i=i, pr=pr):
                    s = AE_S(s)
                    ws.need(s, tok_v_stats[i])
                    s.activation(mv[pr][:, 3:4], mv[pr][:, 2:3], AF.Ln)
                    s.activation(mv[pr][:, 4:5], mv[pr][:, 3:4], AF.Exp, scale=-0.5)
                    tok_s_r[i] = S.tok

                @block.vector
                def _(v, i=i, sl4=sl4, pr=pr):
                    v = AE_V(v)
                    wvv.need(v, tok_s_r[i])
                    if i >= 2:
                        wvv.need(v, tok_t_tr[i - 2])
                    tok_v_xn[i] = V.inc(v.tensor_scalar(
                        xnf[pr][:], xbuf[:, sl4 * DM:(sl4 + 1) * DM],
                        mv[pr][:, 0:1], mv[pr][:, 4:5], ALU.subtract, ALU.mult))

                @block.tensor
                def _(t, i=i, pr=pr):
                    wt.need(t, tok_v_xn[i])
                    wt.need(t, tok_setup_y)
                    if i >= 2:
                        wt.need(t, tok_v_evac[i - 2])
                        wt.need(t, tok_s_stage[i - 2])
                    ins = None
                    for kt in range(KT):
                        ins = t.transpose(trps[pr][:, kt * P:(kt + 1) * P],
                                          xnf[pr][:, kt * P:(kt + 1) * P], identsb[:])
                    tok_t_tr[i] = T.inc(ins)

                @block.vector
                def _(v, i=i, sl=i % 2, pr=pr):
                    v = AE_V(v)
                    wvv.need(v, tok_t_tr[i])
                    if i >= 2:
                        wvv.need(v, tok_y_scr[i - 2])
                    for kt in range(KT):
                        v.tensor_copy(
                            xnstg[:, sl * KT * P + kt * P: sl * KT * P + (kt + 1) * P],
                            trps[pr][:, kt * P:(kt + 1) * P])
                    tok_s_stage[i] = V.tok

                @block.sync
                def _(y, i=i, scol=scol, sl=i % 2):
                    wy.need(y, tok_s_stage[i])
                    tok_y_scr[i] = D["yscr%d" % sl].inc(y.dma_start(
                        bass.AP(xnscr, scol, [[SCRW, P], [P * SCRW, KT], [1, P]]),
                        bass.AP(xnstg, sl * KT * P, [[2 * KT * P, P], [P, KT], [1, P]])))

                if i >= 2:
                    @block.gpsimd
                    def _(g):
                        wg.need(g, V.tok)
                    emit_cast_pieces(2)

                if not fh:
                    @block.vector
                    def _(v, i=i, pr=pr, prev_own2=prev_own2):
                        v = AE_V(v)
                        wvv.need(v, tok_t_tr[i])
                        if prev_own2 is not None:
                            wvv.need(v, tok_t_rt[prev_own2])
                        for kt in range(KT):
                            v.tensor_copy(xnf32b[pr][:, kt * P:(kt + 1) * P],
                                          trps[pr][:, kt * P:(kt + 1) * P])
                        tok_v_evac[i] = V.tok

                    @block.tensor
                    def _(t, i=i, pr=pr, prev_own=prev_own):
                        wt.need(t, tok_v_evac[i])
                        if prev_own is not None:
                            wt.need(t, tok_v_se[prev_own])
                        ins = None
                        for kt in range(KT):
                            ins = t.matmul(lps[:, 0:NE],
                                           xnf32b[pr][:, kt * P:(kt + 1) * P],
                                           gwsb[:, kt * NE:(kt + 1) * NE],
                                           start=(kt == 0), stop=(kt == KT - 1))
                        tok_t_rt[i] = T.inc(ins)

                    @block.vector
                    def _(v, i=i, pr=pr):
                        v = AE_V(v)
                        wvv.need(v, tok_t_rt[i])
                        v.reduce_max(mv[pr][:, 5:6], lps[:, 0:NE], axis=AX.X)
                        V.inc(v.reduce_max(mv[pr][:, 6:7], lps[:, 0:NE], axis=AX.X,
                                           negate=True))

                    @block.scalar
                    def _(s, i=i, pr=pr):
                        s = AE_S(s)
                        ws.need(s, V.tok)
                        S.inc(s.activation(expb[:], lps[:, 0:NE], AF.Exp,
                                           bias=mv[pr][:, 6:7]))

                    @block.vector
                    def _(v, i=i, ri=ri, pr=pr):
                        v = AE_V(v)
                        wvv.need(v, S.tok)
                        v.reduce_sum(mv[pr][:, 7:8], expb[:], axis=AX.X)
                        v.reciprocal(mv[pr][:, 7:8], mv[pr][:, 7:8])
                        tok_v_se[i] = V.inc(v.tensor_scalar(
                            se_tm[:, ri * NE:(ri + 1) * NE], lps[:, 0:NE],
                            mv[pr][:, 5:6], mv[pr][:, 7:8], ALU.is_equal, ALU.mult))

                    @block.tensor
                    def _(t, i=i, ri=ri, prev_own=prev_own):
                        wt.need(t, tok_v_se[i])
                        if prev_own is not None:
                            wt.need(t, tok_s_seem[prev_own])
                        tok_t_se[i] = T.inc(t.transpose(
                            seps[:, 0:P], se_tm[:, ri * NE:(ri + 1) * NE], identsb[:]))

                    @block.vector
                    def _(v, i=i, ri=ri):
                        v = AE_V(v)
                        wvv.need(v, tok_t_se[i])
                        v.tensor_copy(se_em[:, ri * P:(ri + 1) * P], seps[:, 0:P])
                        tok_s_seem[i] = V.tok

            @block.vector
            def _(v):
                v = AE_V(v)
                wvv.need(v, S.tok)
                wvv.need(v, T.tok)
                V.inc(v.tensor_copy(consts[:, 1:2], consts[:, 1:2]))
            fence1 = V.tok
            for blk in (block.tensor, block.scalar, block.sync, block.gpsimd):
                @blk
                def _(e):
                    e.wait_ge(sem_v, fence1[1])
            for w_ in (wt, ws, wy, wg):
                w_.bump(fence1)
            tok_scr_hold = ((tok_y_scr[NT2 - 2], tok_y_scr[NT2 - 1])
                            if NT2 >= 2 else (None, None))

        with contextlib.ExitStack() as ph:
            psb = lambda nm, shp, dt=F32: ph.enter_context(nc.sbuf_tensor(nm, shp, dt))
            xna = psb("a_xn", [P, KT * ACH], BF16)
            t_rep = psb("a_t", [P, ACH])
            rs_rep = psb("a_rs", [P, ACH], BF16)
            bphi = psb("a_bph", [P, ACH])
            bnphi = psb("a_bnp", [P, ACH])
            bthe = psb("a_bth", [P, ACH])
            bnthe = psb("a_bnt", [P, ACH])
            costh = psb("a_cth", [P, ACH])
            sinth = psb("a_sth", [P, ACH])
            cosph = psb("a_cph", [P, ACH + 1])
            sinph = psb("a_sph", [P, ACH + 1])
            pbuf = psb("a_p", [P, ACH])
            vsb = psb("a_vs", [P, 2 * ACH], BF16)
            cbuf = psb("a_cb", [P, ACH])
            tmpA = psb("a_tA", [P, ACH])
            tmpB = psb("a_tB", [P, ACH])
            zfm = psb("a_z", [P, KT * ACH], BF16)
            wkb = psb("a_wk", [P, KT * P], BF16)
            wvb = psb("a_wv", [P, KT * P], BF16)
            wob = psb("a_wo", [P, KT * MMW], BF16)
            xnf = psb("f_xn", [P, 2 * KT * FCH], BF16)
            w2ring = psb("f_w2", [P, W2SLOTS * KCG * MMW], BF16)
            hsb = psb("f_h", [P, NCO * FCH], BF16)
            hps = [ph.enter_context(nc.psum_tensor("f_h%d" % k_, [P, FCH], F32))
                   for k_ in range(NHPS)]
            yps = [ph.enter_context(nc.psum_tensor("f_y%d" % k_, [P, MMW], F32))
                   for k_ in range(NTT)]
            kvp = [ph.enter_context(nc.psum_tensor("a_kv%d" % k_, [P, ACH], F32))
                   for k_ in range(2)]

            zcol_ap = lambda n: bass.AP(consts, 2, [[4, P], [0, n]])

            chunks = ([("fh", c) for c in range(NACH)] if phases >= 2 else []) + \
                     ([("own", c) for c in range(NACH)] if phases >= 3 else [])
            tok_y_wk = {}
            tok_y_wv = {}
            tok_t_kv = {}
            tok_s_A = {}
            tok_s_B = {}
            tok_v_rr = {}
            tok_v_p4 = {}
            tok_v_z = {}
            tok_y_xn = [None] * len(chunks)
            tok_y_trep = [None] * len(chunks)
            tok_t_wo = {}
            tok_v_aacc = {}
            tok_g_wo = [None] * len(chunks)
            tok_v_acc = {}

            def au_chunk_head(ci):
                force_casts_until(lambda: tok_cast_kv)
                kind, c = chunks[ci]
                own = kind == "own"
                scol0 = (TOWN if own else 0) + c * ACH

                @block.sync
                def _(y, ci=ci, own=own, c=c, scol0=scol0):
                    if ci == 0:
                        wy.need(y, tok_scr_hold[0])
                        wy.need(y, tok_scr_hold[1])
                    if ci > 0:
                        wy.need(y, tok_t_kv[(ci - 1, KT - 1)])
                    tok_y_xn[ci] = D["yxna"].inc(y.dma_start(
                        bass.AP(xna, 0, [[KT * ACH, P], [ACH, KT], [1, ACH]]),
                        bass.AP(xnscr, scol0, [[SCRW, P], [P * SCRW, KT], [1, ACH]])))
                    if ci > 0:
                        wy.need(y, tok_s_A[(ci - 1, KT - 1)])
                        wy.need(y, tok_v_z[(ci - 1, KT - 1)])
                    trow = trow_own if own else trow_fh
                    D["ytr"].inc(y.dma_start(
                        t_rep[:], bass.AP(trow, c * ACH, [[0, P], [1, ACH]])))
                    tok_y_trep[ci] = D["ytr"].inc(y.dma_start(
                        rs_rep[:], bass.AP(rsrow, c * ACH, [[0, P], [1, ACH]])))


            def au_chtile(ci, j):
                kind, c = chunks[ci]
                own = kind == "own"
                gj = ci * KT + j

                @block.gpsimd
                def _(g, ci=ci, j=j, gj=gj):
                    wg.need(g, tok_cast_kv)
                    if gj >= 1:
                        pci, pjj = (gj - 1) // KT, (gj - 1) % KT
                        wg.need(g, tok_t_kv[(pci, pjj)])
                    tok_y_wk[(ci, j)] = D["gwk"].inc(g.dma_start(
                        bass.AP(wkb, 0, [[KT * P, P], [1, KT * P]]),
                        bass.AP(wkc, j * P * KT * P, [[KT * P, P], [1, KT * P]])))
                    tok_y_wv[(ci, j)] = D["gwv"].inc(g.dma_start(
                        bass.AP(wvb, 0, [[KT * P, P], [1, KT * P]]),
                        bass.AP(wvc, j * P * KT * P, [[KT * P, P], [1, KT * P]])))

                @block.tensor
                def _(t, ci=ci, j=j, gj=gj):
                    wt.need(t, tok_y_wk[(ci, j)])
                    wt.need(t, tok_y_wv[(ci, j)])
                    wt.need(t, tok_y_xn[ci])
                    if gj >= 1:
                        pci, pjj = (gj - 1) // KT, (gj - 1) % KT
                        wt.need(t, tok_s_A[(pci, pjj)])
                    if j == 0:
                        for hh2 in range(NH):
                            if (ci - 1, hh2, ATT - 1) in tok_v_aacc:
                                wt.need(t, tok_v_aacc[(ci - 1, hh2, ATT - 1)])
                    ins = None
                    for kt in range(KT):
                        t.matmul(kvp[0][:, 0:ACH],
                                 wkb[:, kt * P:(kt + 1) * P],
                                 xna[:, kt * ACH:(kt + 1) * ACH],
                                 start=(kt == 0), stop=(kt == KT - 1))
                    for kt in range(KT):
                        ins = t.matmul(kvp[1][:, 0:ACH],
                                       wvb[:, kt * P:(kt + 1) * P],
                                       xna[:, kt * ACH:(kt + 1) * ACH],
                                       start=(kt == 0), stop=(kt == KT - 1))
                    tok_t_kv[(ci, j)] = T.inc(ins)

                @block.scalar
                def _(s, ci=ci, j=j, gj=gj):
                    s = AE_S(s)
                    ws.need(s, tok_t_kv[(ci, j)])
                    ws.need(s, tok_y_trep[ci])
                    if gj >= 1:
                        pci, pjj = (gj - 1) // KT, (gj - 1) % KT
                        ws.need(s, tok_v_rr[(pci, pjj)])
                    if gj >= 2:
                        pci, pjj = (gj - 2) // KT, (gj - 2) % KT
                        ws.need(s, tok_v_p4[(pci, pjj)])
                    s.activation(vsb[:, (gj % 2) * ACH:(gj % 2 + 1) * ACH],
                                 kvp[1][:, 0:ACH], AF.Identity, scale=1.0)
                    s.activation(bphi[:], kvp[0][:, 0:ACH], AF.Identity, scale=10.0)
                    s.activation(bnphi[:], kvp[0][:, 0:ACH], AF.Identity,
                                 bias=consts[:, 0:1], scale=10.0 * INV_2PI)
                    s.activation(bthe[:], t_rep[:], AF.Identity, scale=fcol[:, j:j + 1])
                    s.activation(bnthe[:], t_rep[:], AF.Identity,
                                 bias=consts[:, 0:1], scale=f2col[:, j:j + 1])
                    tok_s_A[(ci, j)] = S.tok

                @block.vector
                def _(v, ci=ci, j=j, gj=gj):
                    v = AE_V(v)
                    wvv.need(v, tok_s_A[(ci, j)])
                    v.tensor_scalar(tmpA[:], bnphi[:], -MAGIC, -TWO_PI,
                                    ALU.add, ALU.mult)
                    v.tensor_tensor(bphi[:], bphi[:], tmpA[:], ALU.add)
                    v.tensor_scalar(bphi[:], bphi[:], PI_LO, -PI_LO, ALU.min, ALU.max)
                    v.tensor_scalar(tmpA[:], bnthe[:], -MAGIC, -TWO_PI,
                                    ALU.add, ALU.mult)
                    v.tensor_tensor(bthe[:], bthe[:], tmpA[:], ALU.add)
                    v.tensor_scalar(bthe[:], bthe[:], PI_LO, -PI_LO, ALU.min, ALU.max)
                    tok_v_rr[(ci, j)] = V.tok

                @block.scalar
                def _(s, ci=ci, j=j, gj=gj):
                    s = AE_S(s)
                    ws.need(s, tok_v_rr[(ci, j)])
                    if gj >= 1:
                        pci, pjj = (gj - 1) // KT, (gj - 1) % KT
                        ws.need(s, tok_v_z[(pci, pjj)])
                    s.activation(sinph[:, 1:ACH + 1], bphi[:], AF.Sin)
                    s.activation(bnphi[:], bphi[:], AF.Abs)
                    s.activation(cosph[:, 1:ACH + 1], bnphi[:], AF.Sin,
                                 bias=consts[:, 1:2], scale=-1.0)
                    s.activation(sinth[:], bthe[:], AF.Sin)
                    s.activation(bnthe[:], bthe[:], AF.Abs)
                    s.activation(costh[:], bnthe[:], AF.Sin,
                                 bias=consts[:, 1:2], scale=-1.0)
                    tok_s_B[(ci, j)] = S.tok

                @block.vector
                def _(v, ci=ci, j=j, own=own, c=c):
                    v = AE_V(v)
                    wvv.need(v, tok_s_B[(ci, j)])
                    wvv.need(v, tok_setup_v)
                    v.tensor_copy(cosph[:, 0:1], bndc[:, 2 * j:2 * j + 1])
                    v.tensor_copy(sinph[:, 0:1], bndc[:, 2 * j + 1:2 * j + 2])
                    v.tensor_copy(bndc[:, 2 * j:2 * j + 1], cosph[:, ACH:ACH + 1])
                    v.tensor_copy(bndc[:, 2 * j + 1:2 * j + 2], sinph[:, ACH:ACH + 1])
                    binders = [costh[:, 0:ACH], sinth[:, 0:ACH],
                               cosph[:, 0:ACH], sinph[:, 0:ACH]]
                    probes = [costh[:, 0:ACH], sinth[:, 0:ACH],
                              cosph[:, 1:ACH + 1], sinph[:, 1:ACH + 1]]
                    if own and (ci - 1, 1, ATT - 1) in tok_t_wo:
                        wvv.need(v, tok_t_wo[(ci - 1, 1, ATT - 1)])
                    gj = ci * KT + j
                    for sn in range(4):
                        v.tensor_tensor(pbuf[:],
                                        vsb[:, (gj % 2) * ACH:(gj % 2 + 1) * ACH],
                                        binders[sn], ALU.mult)
                        if sn == 3:
                            tok_v_p4[(ci, j)] = V.tok
                        v.tensor_tensor_scan(
                            cbuf[:], pbuf[:], zcol_ap(ACH),
                            carries[:, 4 * j + sn:4 * j + sn + 1],
                            ALU.add, ALU.add)
                        v.tensor_copy(carries[:, 4 * j + sn:4 * j + sn + 1],
                                      cbuf[:, ACH - 1:ACH])
                        if own:
                            if sn == 0:
                                v.tensor_tensor(tmpA[:], cbuf[:], probes[0], ALU.mult)
                            elif sn == 1:
                                v.tensor_tensor(tmpB[:], cbuf[:], probes[1], ALU.mult)
                                v.tensor_tensor(tmpA[:], tmpA[:], tmpB[:], ALU.add)
                            elif sn == 2:
                                v.tensor_tensor(tmpB[:], cbuf[:], probes[2], ALU.mult)
                            else:
                                v.tensor_tensor(pbuf[:], cbuf[:], probes[3], ALU.mult)
                    if own:
                        v.tensor_tensor(tmpB[:], tmpB[:], pbuf[:], ALU.add)
                        v.tensor_scalar_mul(tmpA[:], tmpA[:], gposc[:, j:j + 1])
                        v.scalar_tensor_tensor(tmpA[:], tmpB[:], gac[:, j:j + 1],
                                               tmpA[:], ALU.mult, ALU.add)
                        wvv.need(v, tok_y_trep[ci])
                        v.tensor_tensor(zfm[:, j * ACH:(j + 1) * ACH], tmpA[:],
                                        rs_rep[:], ALU.mult)
                    tok_v_z[(ci, j)] = V.tok

            def au_bmask():
                @block.vector
                def _(v):
                    v = AE_V(v)
                    v.tensor_scalar_mul(carries[:], carries[:], bmaskc[:, 0:1])
                    v.tensor_scalar_mul(bndc[:], bndc[:], bmaskc[:, 0:1])

            def au_wo(ci, hh):
                kind, c = chunks[ci]

                @block.gpsimd
                def _(y, ci=ci, hh=hh):
                    wg.need(y, tok_cast_kv)
                    if hh == 0 and (ci - 1, 1, ATT - 1) in tok_t_wo:
                        wg.need(y, tok_t_wo[(ci - 1, 1, ATT - 1)])
                    if hh == 1:
                        wg.need(y, tok_t_wo[(ci, 0, ATT - 1)])
                    tok_g_wo[ci] = D["gwo"].inc(y.dma_start(
                        bass.AP(wob, 0, [[KT * MMW, P], [1, KT * MMW]]),
                        bass.AP(woc, hh * P * KT * MMW,
                                [[KT * MMW, P], [1, KT * MMW]])))

                for tt in range(ATT):
                    ttg = c * ATT + tt

                    @block.tensor
                    def _(t, ci=ci, c=c, tt=tt, ttg=ttg, hh=hh):
                        for jj in range(KT):
                            wt.need(t, tok_v_z[(ci, jj)])
                        wt.need(t, tok_g_wo[ci])
                        wt.need(t, tok_b2)
                        wt.need(t, tok_s_A[(ci, KT - 1)])
                        if (ci, hh, tt - 1) in tok_v_aacc:
                            wt.need(t, tok_v_aacc[(ci, hh, tt - 1)])
                        ins = None
                        for jj in range(KT):
                            t.matmul(kvp[hh][:, 0:MMW],
                                     zfm[:, jj * ACH + tt * P: jj * ACH + (tt + 1) * P],
                                     wob[:, jj * MMW:(jj + 1) * MMW],
                                     start=(jj == 0), stop=False)
                        ins = t.matmul(
                            kvp[hh][:, 0:MMW],
                            bass.AP(se_em, c * ACH + tt * P, [[TOWN, NE], [1, P]]),
                            bass.AP(b2sb, hh * MMW, [[DM, NE], [1, MMW]]),
                            start=False, stop=True)
                        tok_t_wo[(ci, hh, tt)] = T.inc(ins)

                    @block.vector
                    def _(v, ci=ci, c=c, tt=tt, ttg=ttg, hh=hh):
                        v = AE_V(v)
                        wvv.need(v, tok_t_wo[(ci, hh, tt)])
                        if (0, c, tt) in tok_v_acc:
                            wvv.need(v, tok_v_acc[(0, c, tt)])
                        v.scalar_tensor_tensor(
                            accf[:, ttg * DM + hh * MMW: ttg * DM + (hh + 1) * MMW],
                            kvp[hh][:, 0:MMW], 1.0,
                            accf[:, ttg * DM + hh * MMW: ttg * DM + (hh + 1) * MMW],
                            ALU.mult, ALU.add)
                        tok_v_aacc[(ci, hh, tt)] = V.tok

            attn_units = []
            for ci in range(len(chunks)):
                kind, c = chunks[ci]
                attn_units.append(lambda ci=ci: au_chunk_head(ci))
                for j in range(KT):
                    attn_units.append(lambda ci=ci, j=j: au_chtile(ci, j))
                if kind == "fh" and ci == NACH - 1:
                    attn_units.append(au_bmask)
                if kind == "own":
                    for hh in range(NH):
                        attn_units.append(lambda ci=ci, hh=hh: au_wo(ci, hh))
            au_idx = [0]

            def emit_attn_units(upto):
                while au_idx[0] < min(upto, len(attn_units)):
                    attn_units[au_idx[0]]()
                    au_idx[0] += 1

            tok_g_w1 = {}
            tok_g_w2p = {}
            tok_t_f1 = {}
            tok_t_f1ct = {}
            tok_s_h = {}
            tok_t_w2free = {}
            tok_t_f2stop = {}
            tok_t_hsbfree = {}
            tok_v_yev = {}
            tok_y_fxn = {}
            NCH = (NE * NFCH) if phases >= 4 else 0
            gpc = [0]

            def emit_w2_pieces(e, hh, up0):
                force_casts_until(lambda: tok_cast_w2[e])
                for kcg in range(NKCG):
                    up = up0 + kcg
                    slot = up % W2SLOTS

                    @block.gpsimd
                    def _(g, e=e, hh=hh, kcg=kcg, up=up, slot=slot):
                        if up >= W2SLOTS:
                            wg.need(g, tok_t_w2free[up - W2SLOTS])
                        wg.need(g, tok_cast_w2[e])
                        tok_g_w2p[up] = D["gp%d" % slot].inc(g.dma_start(
                            bass.AP(w2ring, slot * KCG * MMW,
                                    [[W2SLOTS * KCG * MMW, P], [1, KCG * MMW]]),
                            bass.AP(w2c,
                                    ((e * NH + hh) * NKCG + kcg) * P * KCG * MMW,
                                    [[KCG * MMW, P], [1, KCG * MMW]])))

            SLOTS_TOTAL = max(1, NCH * (NCOB + NH))
            au_slot = [0]

            def au_tick():
                au_slot[0] += 1
                emit_attn_units(au_slot[0] * len(attn_units) // SLOTS_TOTAL)

            for gi in range(NCH):
                e, c = gi // NFCH, gi % NFCH
                sl = c % 2
                gcob0 = gi * NCOB

                emit_w2_pieces(e, 0, gpc[0])
                @block.gpsimd
                def _(g):
                    wg.need(g, T.tok)
                emit_cast_pieces(4)

                @block.sync
                def _(y, e=e, c=c, sl=sl, gi=gi):
                    wy.need(y, tok_scr_hold[0])
                    wy.need(y, tok_scr_hold[1])
                    pgi = gi - 2
                    if pgi >= 0:
                        pk = (pgi // NFCH, pgi % NFCH)
                        if (pk[0], pk[1], NCOB - 1) in tok_t_f1:
                            wy.need(y, tok_t_f1[(pk[0], pk[1], NCOB - 1)])
                    tok_y_fxn[(e, c)] = D["yxnf%d" % sl].inc(y.dma_start(
                        bass.AP(xnf, sl * KT * FCH,
                                [[2 * KT * FCH, P], [FCH, KT], [1, FCH]]),
                        bass.AP(xnscr, TOWN + c * FCH,
                                [[SCRW, P], [P * SCRW, KT], [1, FCH]])))

                force_casts_until(lambda: tok_cast_w1[e])
                for cob in range(NCOB):
                    slot = (gcob0 + cob) % RING

                    @block.sync
                    def _(y, e=e, c=c, cob=cob, slot=slot, gcob0=gcob0):
                        prev = gcob0 + cob - RING
                        if prev >= 0:
                            pe = prev // (NFCH * NCOB)
                            pc = (prev // NCOB) % NFCH
                            pcob = prev % NCOB
                            wy.need(y, tok_t_f1[(pe, pc, pcob)])
                        wy.need(y, tok_cast_w1[e])
                        tok_g_w1[gcob0 + cob] = D["yw1%d" % slot].inc(y.dma_start(
                            bass.AP(w1ring, slot * KT * W1CO,
                                    [[RING * KT * W1CO, P], [1, KT * W1CO]]),
                            bass.AP(w1c, (e * NCOB + cob) * P * KT * W1CO,
                                    [[KT * W1CO, P], [1, KT * W1CO]])))

                    def emit_f1_mms(ctb, e=e, c=c, cob=cob, slot=slot, sl=sl,
                                    gcob0=gcob0):
                        @block.tensor
                        def _(t, ctb=ctb):
                            wt.need(t, tok_g_w1[gcob0 + cob])
                            wt.need(t, tok_y_fxn[(e, c)])
                            for ct in range(ctb, min(ctb + NHPS, NW1C)):
                                ctg = cob * NW1C + ct
                                pr = ctg - NHPS
                                if pr >= 0:
                                    wt.need(t, tok_s_h[(e, c, pr)])
                                else:
                                    pk = (e, c - 1) if c > 0 else (e - 1, NFCH - 1)
                                    if pk[0] >= 0:
                                        wt.need(t, tok_s_h[(pk[0], pk[1],
                                                            NCO + pr)])
                                ins = None
                                for kt in range(KT):
                                    ins = t.matmul(
                                        hps[ctg % NHPS][:, 0:FCH],
                                        w1ring[:, slot * KT * W1CO + kt * W1CO
                                               + ct * P:
                                               slot * KT * W1CO + kt * W1CO
                                               + (ct + 1) * P],
                                        xnf[:, sl * KT * FCH + kt * FCH:
                                            sl * KT * FCH + (kt + 1) * FCH],
                                        start=(kt == 0), stop=(kt == KT - 1))
                                tok_t_f1ct[(e, c, ctg)] = T.inc(ins)

                    def emit_f1_gelus(e=e, c=c, cob=cob, gi=gi):
                        base = T.n

                        @block.scalar
                        def _(s):
                            s = AE_S(s)
                            ws.need(s, tok_setup_y)
                            pgi = gi - 1
                            if pgi >= 0:
                                for grp in sorted({(cob * NW1C + ct) // KCG
                                                   for ct in range(NW1C)}):
                                    ws.need(s, tok_t_hsbfree[(pgi, grp)])
                            for ct in range(NW1C):
                                ctg = cob * NW1C + ct
                                if (e, c, ctg) in tok_t_f1ct:
                                    ws.need(s, tok_t_f1ct[(e, c, ctg)])
                                else:
                                    ws.need(s, (T, base + ct - NHPS + 1))
                                ins = s.activation(
                                    hsb[:, ctg * FCH:(ctg + 1) * FCH],
                                    hps[ctg % NHPS][:, 0:FCH], AF.Gelu,
                                    bias=b1sb[:, e * NCO + ctg:
                                              e * NCO + ctg + 1])
                                tok_s_h[(e, c, ctg)] = S.inc(ins)

                    emit_f1_mms(0)
                    emit_f1_gelus()
                    for ctb in range(NHPS, NW1C, NHPS):
                        emit_f1_mms(ctb)
                    tok_t_f1[(e, c, cob)] = (T, T.n)
                    au_tick()

                for hh in range(NH):
                    up0 = gpc[0]
                    uh = gi * NH + hh

                    @block.tensor
                    def _(t, e=e, c=c, hh=hh, gi=gi, up0=up0, uh=uh):
                        for kcg in range(NKCG):
                            up = up0 + kcg
                            slot = up % W2SLOTS
                            wt.need(t, tok_g_w2p[up])
                            wt.need(t, tok_s_h[(e, c, kcg * KCG + KCG - 1)])
                            if kcg == 0 and uh >= 1:
                                for tt in range(NTT):
                                    wt.need(t, tok_v_yev[(uh - 1, tt)])
                            ins = None
                            for tt in range(NTT):
                                for k2 in range(KCG):
                                    kc = kcg * KCG + k2
                                    ins = t.matmul(
                                        yps[tt][:, 0:MMW],
                                        hsb[:, kc * FCH + tt * P:
                                            kc * FCH + (tt + 1) * P],
                                        w2ring[:, slot * KCG * MMW + k2 * MMW:
                                               slot * KCG * MMW
                                               + (k2 + 1) * MMW],
                                        start=(kcg == 0 and k2 == 0),
                                        stop=(kcg == NKCG - 1
                                              and k2 == KCG - 1))
                                if kcg == NKCG - 1:
                                    tok_t_f2stop[(uh, tt)] = T.inc(ins)
                            tok_t_w2free[up] = T.inc(ins)
                            if hh == NH - 1:
                                tok_t_hsbfree[(gi, kcg)] = tok_t_w2free[up]
                    gpc[0] += NKCG
                    if hh + 1 < NH:
                        emit_w2_pieces(e, hh + 1, gpc[0])
                    elif gi + 1 < NCH:
                        pass

                    for tt in range(NTT):
                        ttg = c * NTT + tt

                        @block.vector
                        def _(v, e=e, c=c, tt=tt, ttg=ttg, hh=hh, uh=uh):
                            v = AE_V(v)
                            wvv.need(v, tok_t_f2stop[(uh, tt)])
                            dst = accf[:, ttg * DM + hh * MMW:
                                       ttg * DM + (hh + 1) * MMW]
                            if e == 0:
                                v.tensor_scalar(
                                    dst, yps[tt][:, 0:MMW],
                                    se_tm[:, ttg * NE + e:ttg * NE + e + 1],
                                    None, ALU.mult)
                            else:
                                v.scalar_tensor_tensor(
                                    dst, yps[tt][:, 0:MMW],
                                    se_tm[:, ttg * NE + e:ttg * NE + e + 1],
                                    dst, ALU.mult, ALU.add)
                            tok_v_yev[(uh, tt)] = V.tok
                            if e == 0:
                                tok_v_acc[(0, c, tt)] = V.tok
                    au_tick()

            emit_attn_units(len(attn_units))
            finish_casts()

            @block.vector
            def _(v):
                v = AE_V(v)
                wvv.need(v, T.tok)
                wvv.need(v, S.tok)
                V.inc(v.tensor_copy(consts[:, 1:2], consts[:, 1:2]))
            fence3 = V.tok
            for blk in (block.tensor, block.scalar, block.sync, block.gpsimd):
                @blk
                def _(e):
                    e.wait_ge(sem_v, fence3[1])
            for w_ in (wt, ws, wy, wg):
                w_.bump(fence3)

        with contextlib.ExitStack() as ph:
            psb = lambda nm, shp, dt=F32: ph.enter_context(nc.sbuf_tensor(nm, shp, dt))
            xb2 = psb("o_x", [P, 2 * DM])
            ob = psb("o_o", [P, 2 * DM])
            tok_y_x2 = [None] * NLT
            tok_v_o = [None] * NLT
            tok_y_out = [None] * NLT

            for i in range(NLT):
                sl = i % 2

                @block.sync
                def _(y, i=i, sl=sl):
                    if i >= 2:
                        wy.need(y, tok_v_o[i - 2])
                    tok_y_x2[i] = D["yx%d" % sl].inc(y.dma_start(
                        xb2[:, sl * DM:(sl + 1) * DM], xown[i * P:(i + 1) * P, :]))

                @block.vector
                def _(v, i=i, sl=sl):
                    v = AE_V(v)
                    wvv.need(v, tok_y_x2[i])
                    if i >= 2:
                        wvv.need(v, tok_y_out[i - 2])
                    tok_v_o[i] = V.inc(v.tensor_tensor(
                        ob[:, sl * DM:(sl + 1) * DM],
                        xb2[:, sl * DM:(sl + 1) * DM],
                        accf[:, i * DM:(i + 1) * DM], ALU.add))

                @block.sync
                def _(y, i=i, sl=sl):
                    wy.need(y, tok_v_o[i])
                    tok_y_out[i] = D["yout%d" % sl].inc(y.dma_start(
                        out_d[i * P:(i + 1) * P, :], ob[:, sl * DM:(sl + 1) * DM]))

            @block.sync
            def _(y):
                y.wait_ge(D["yout0"].sem, D["yout0"].n)
                y.wait_ge(D["yout1"].sem, D["yout1"].n)
            @block.gpsimd
            def _(g):
                g.wait_ge(D["yout0"].sem, D["yout0"].n)
                g.wait_ge(D["yout1"].sem, D["yout1"].n)

    return nc


_NC_CACHE = {}


def _get_nc():
    if "full" not in _NC_CACHE:
        _NC_CACHE["full"] = build_program(Cfg())
    return _NC_CACHE["full"]


def host_inputs(cfg, core, x_full, Wk, Wv, Wo, gate, freqs, gate_w, w1, b1, w2, b2):
    b, half = core // 2, core % 2
    TOWN = cfg.TOWN
    T_batch = 2 * TOWN
    Dh = freqs.shape[1]
    f32 = np.float32
    own0 = half * TOWN
    fflat = np.ascontiguousarray(freqs.reshape(-1))
    gpos = np.repeat(gate[:, 0], Dh)
    gassoc = np.repeat(gate[:, 1], Dh)
    col = lambda vec: np.ascontiguousarray(vec.reshape(cfg.KT, P).T).astype(f32)
    b1t = np.ascontiguousarray(
        b1.reshape(cfg.NE, cfg.NCO, P).transpose(2, 0, 1).reshape(P, cfg.NE * cfg.NCO))
    tg = np.arange(T_batch, dtype=f32)
    return {
        "xown": np.ascontiguousarray(x_full[b, own0:own0 + TOWN]),
        "xfh": np.ascontiguousarray(x_full[b, 0:TOWN]),
        "wk": Wk, "wv": Wv, "wo": Wo, "gate_w": gate_w,
        "w1": w1, "w2": w2, "b2": b2, "b1t": b1t,
        "fcol": col(fflat), "gpos": col(gpos), "gassoc": col(gassoc),
        "trow_own": np.ascontiguousarray(tg[None, own0:own0 + TOWN]),
        "trow_fh": np.ascontiguousarray(tg[None, 0:TOWN]),
        "rsrow": np.ascontiguousarray(
            (1.0 / np.sqrt(tg[own0:own0 + TOWN] + 1.0))[None, :]).astype(
                ml_dtypes.bfloat16),
        "ident": np.eye(P, dtype=f32),
        "bmask": np.full((P, 1), float(half), dtype=f32),
    }


def kernel(L0_attn_x, L0_attn_Wk, L0_attn_Wv, L0_attn_Wo, L0_attn_gate,
           L0_attn_freqs, L0_ffn_gate_w, L0_ffn_w1, L0_ffn_b1, L0_ffn_w2,
           L0_ffn_b2):
    cfg = Cfg()
    nc = _get_nc()
    f32 = np.float32
    x = np.asarray(L0_attn_x, f32)
    args = (np.asarray(L0_attn_Wk, f32), np.asarray(L0_attn_Wv, f32),
            np.asarray(L0_attn_Wo, f32), np.asarray(L0_attn_gate, f32),
            np.asarray(L0_attn_freqs, f32), np.asarray(L0_ffn_gate_w, f32),
            np.asarray(L0_ffn_w1, f32), np.asarray(L0_ffn_b1, f32),
            np.asarray(L0_ffn_w2, f32), np.asarray(L0_ffn_b2, f32))
    in_maps = [host_inputs(cfg, c, x, *args) for c in range(8)]
    res = run_bass_kernel_spmd(nc, in_maps, core_ids=list(range(8)))
    B, Tt, DMm = x.shape
    out = np.empty((B, Tt, DMm), f32)
    for c in range(8):
        b, half = c // 2, c % 2
        out[b, half * cfg.TOWN:(half + 1) * cfg.TOWN] = res.results[c]["out"]
    return out


# version 18
# speedup vs baseline: 1.0835x; 1.0835x over previous
import contextlib
import numpy as np
import ml_dtypes
import concourse.bass as bass
import concourse.mybir as mybir
from concourse.bass_utils import run_bass_kernel_spmd

F32 = mybir.dt.float32
BF16 = mybir.dt.bfloat16
AF = mybir.ActivationFunctionType
ALU = mybir.AluOpType
AX = mybir.AxisListType

P = 128
MAGIC = 12582912.0
TWO_PI = 6.2831854820251465
INV_2PI = 0.15915493667125702
PI_LO = 3.1415925025939941
PI_HALF = 1.5707963267948966


class Cfg:
    def __init__(self, KT=8, NCO=32, NE=8, NLT=16, ACH=512, FCH=512, W1CO=512,
                 RING=2):
        self.KT = KT
        self.DM = KT * P
        self.NCO = NCO
        self.DFF = NCO * P
        self.NE = NE
        self.NLT = NLT
        self.TOWN = NLT * P
        self.ACH = ACH
        self.NACH = self.TOWN // ACH
        self.ATT = ACH // P
        self.FCH = FCH
        self.NTT = FCH // P
        self.NFCH = self.TOWN // FCH
        self.W1CO = W1CO
        self.NW1C = W1CO // P
        self.NCOB = self.DFF // W1CO
        self.RING = RING


class Trk:
    def __init__(self, sem, step=1):
        self.sem, self.n, self.step = sem, 0, step

    def __init_seen(self):
        if not hasattr(self, "_seen"):
            self._seen = set()

    def inc(self, instr):
        self.__init_seen()
        if id(instr) not in self._seen:
            self._seen.add(id(instr))
            self._keep = getattr(self, "_keep", [])
            self._keep.append(instr)
            instr.then_inc(self.sem, self.step)
            self.n += self.step
        return (self, self.n)

    @property
    def tok(self):
        return (self, self.n)


class Waiter:
    def __init__(self):
        self.wm = {}

    def need(self, eng, tok):
        if tok is None:
            return
        if isinstance(tok[0], (tuple, list)):
            for t in tok:
                self.need(eng, t)
            return
        trk, val = tok
        if val is None or val <= 0:
            return
        key = id(trk)
        if self.wm.get(key, 0) < val:
            eng.wait_ge(trk.sem, val)
            self.wm[key] = val

    def bump(self, tok):
        trk, val = tok
        key = id(trk)
        self.wm[key] = max(self.wm.get(key, 0), val)


_AE_OPS = {
    "memset", "memzero", "tensor_copy", "tensor_tensor", "tensor_add",
    "tensor_sub", "tensor_mul", "tensor_max", "tensor_scalar",
    "tensor_single_scalar", "tensor_scalar_add", "tensor_scalar_sub",
    "tensor_scalar_mul", "tensor_scalar_max", "tensor_scalar_min",
    "tensor_relu", "tensor_reduce", "reduce_max", "reduce_sum",
    "scalar_tensor_tensor", "tensor_tensor_scan", "bn_stats", "bn_aggr",
    "reciprocal", "select", "transpose", "activation", "copy", "mul", "add",
    "sqrt", "square", "sign",
}


class AutoEdge:

    def __init__(self, eng, trk, waiter):
        self._e, self._t, self._w = eng, trk, waiter

    def __getattr__(self, nm):
        fn = getattr(self._e, nm)
        if nm in _AE_OPS:
            eng, trk, w = self._e, self._t, self._w

            def wrapped(*a, **k):
                w.need(eng, trk.tok)
                ins = fn(*a, **k)
                trk.inc(ins)
                return ins

            return wrapped
        return fn


def build_program(cfg: Cfg, phases=5):
    nc = bass.Bass("TRN2", target_bir_lowering=False)
    KT, DM, NCO, DFF, NE = cfg.KT, cfg.DM, cfg.NCO, cfg.DFF, cfg.NE
    NLT, TOWN, ACH, NACH, ATT = cfg.NLT, cfg.TOWN, cfg.ACH, cfg.NACH, cfg.ATT
    FCH, NTT, NFCH = cfg.FCH, cfg.NTT, cfg.NFCH
    W1CO, NW1C, NCOB, RING = cfg.W1CO, cfg.NW1C, cfg.NCOB, cfg.RING
    NHPS = 2
    assert cfg.NW1C % NHPS == 0 or cfg.NW1C <= NHPS
    assert cfg.ACH == min(512, cfg.DM)
    MMW = min(512, DM)
    NH = DM // MMW
    KCG = min(4, NCO)
    assert NCO % KCG == 0 and KCG % NHPS == 0
    NKCG = NCO // KCG
    W2SLOTS = min(8, max(1, NKCG * NH))
    assert cfg.NTT + NHPS + 2 <= 8
    AW = min(512, ACH)
    ANH = ACH // AW
    SCRW = 2 * TOWN

    xown = nc.dram_tensor("xown", [TOWN, DM], F32, kind="ExternalInput")
    xfh = nc.dram_tensor("xfh", [TOWN, DM], F32, kind="ExternalInput")
    wk = nc.dram_tensor("wk", [DM, DM], F32, kind="ExternalInput")
    wv = nc.dram_tensor("wv", [DM, DM], F32, kind="ExternalInput")
    wo = nc.dram_tensor("wo", [DM, DM], F32, kind="ExternalInput")
    gate_w = nc.dram_tensor("gate_w", [DM, NE], F32, kind="ExternalInput")
    w1 = nc.dram_tensor("w1", [NE, DM, DFF], F32, kind="ExternalInput")
    w2 = nc.dram_tensor("w2", [NE, DFF, DM], F32, kind="ExternalInput")
    b1t = nc.dram_tensor("b1t", [P, NE * NCO], F32, kind="ExternalInput")
    b2 = nc.dram_tensor("b2", [NE, DM], F32, kind="ExternalInput")
    fcol_d = nc.dram_tensor("fcol", [P, KT], F32, kind="ExternalInput")
    gpos_d = nc.dram_tensor("gpos", [P, KT], F32, kind="ExternalInput")
    gas_d = nc.dram_tensor("gassoc", [P, KT], F32, kind="ExternalInput")
    trow_own = nc.dram_tensor("trow_own", [1, TOWN], F32, kind="ExternalInput")
    trow_fh = nc.dram_tensor("trow_fh", [1, TOWN], F32, kind="ExternalInput")
    rsrow = nc.dram_tensor("rsrow", [1, TOWN], BF16, kind="ExternalInput")
    ident_d = nc.dram_tensor("ident", [P, P], F32, kind="ExternalInput")
    bmask_d = nc.dram_tensor("bmask", [P, 1], F32, kind="ExternalInput")
    out_d = nc.dram_tensor("out", [TOWN, DM], F32, kind="ExternalOutput")
    xnscr = nc.dram_tensor("xnscr", [KT, P, SCRW], BF16)
    wkc = nc.dram_tensor("wkc", [KT * P, KT * P], BF16)
    wvc = nc.dram_tensor("wvc", [KT * P, KT * P], BF16)
    woc = nc.dram_tensor("woc", [NH * P, KT * MMW], BF16)
    w1c = nc.dram_tensor("w1c", [NE * NCOB * P, KT * W1CO], BF16)
    w2c = nc.dram_tensor("w2c", [NE * NH * NKCG * P, KCG * MMW], BF16)

    es = contextlib.ExitStack()
    sbt = lambda nm, shp, dt=F32: es.enter_context(nc.sbuf_tensor(nm, shp, dt))

    accf = sbt("accf", [P, NLT * DM], BF16)
    se_tm = sbt("se_tm", [P, NLT * NE], F32)
    se_em = sbt("se_em", [NE, TOWN], BF16)
    carries = sbt("carries", [P, KT * 4], F32)
    bndc = sbt("bndc", [P, KT * 2], F32)
    consts = sbt("consts", [P, 4], F32)
    fcol = sbt("fcol_s", [P, KT], F32)
    f2col = sbt("f2col_s", [P, KT], F32)
    gposc = sbt("gpos_s", [P, KT], F32)
    gac = sbt("gas_s", [P, KT], F32)
    b1sb = sbt("b1_s", [P, NE * NCO], F32)
    b2sb = sbt("b2_s", [NE, DM], BF16)
    gwsb = sbt("gw_s", [P, KT * NE], F32)
    identsb = sbt("ident_s", [P, P], F32)
    bmaskc = sbt("bmask_s", [P, 1], F32)
    w1ring = sbt("w1ring", [P, RING * KT * W1CO], BF16)

    sem_t = es.enter_context(nc.semaphore("sem_t"))
    sem_v = es.enter_context(nc.semaphore("sem_v"))
    sem_s = es.enter_context(nc.semaphore("sem_s"))
    T = Trk(sem_t); V = Trk(sem_v); S = Trk(sem_s)
    D = {}
    for nm in ("ysetup", "yx0", "yx1", "yscr0", "yscr1", "yxna", "ytr",
               "yxnf0", "yxnf1", "yout0", "yout1",
               "gsetup", "ywk0", "ywv0", "ywo",
               "gp0", "gp1", "gp2", "gp3", "gp4", "gp5", "gp6", "gp7",
               "gcw0", "gcw1", "gcw2", "gcw3",
               "yw10", "yw11", "yw12", "gcast", "gcastw"):
        D[nm] = Trk(es.enter_context(nc.semaphore("d_" + nm)), 16)
    wt, wvv, ws, wg, wy = Waiter(), Waiter(), Waiter(), Waiter(), Waiter()
    AE_V = lambda eng: AutoEdge(eng, V, wvv)
    AE_S = lambda eng: AutoEdge(eng, S, ws)

    block = es.enter_context(nc.Block())

    with es:
        @block.sync
        def _(y):
            D["ysetup"].inc(y.dma_start(fcol[:], fcol_d[:, :]))
            D["ysetup"].inc(y.dma_start(gposc[:], gpos_d[:, :]))
            D["ysetup"].inc(y.dma_start(gac[:], gas_d[:, :]))
            D["ysetup"].inc(y.dma_start(b1sb[:], b1t[:, :]))
            D["ysetup"].inc(y.dma_start(identsb[:], ident_d[:, :]))
            D["ysetup"].inc(y.dma_start(bmaskc[:], bmask_d[:, :]))
            D["ysetup"].inc(y.dma_start(
                bass.AP(gwsb, 0, [[KT * NE, P], [NE, KT], [1, NE]]),
                bass.AP(gate_w, 0, [[NE, P], [NE * P, KT], [1, NE]])))
        tok_setup_y = D["ysetup"].tok

        @block.gpsimd
        def _(g):
            D["gsetup"].inc(g.dma_start(b2sb[:], b2[:, :]))
        tok_b2 = D["gsetup"].tok

        tok_cast_kv = None
        tok_cast_w1 = [None] * NE
        tok_cast_w2 = [None] * NE

        cast_pieces = []
        for _j in range(KT):
            cast_pieces.append(("kv", _j))
        for _hh in range(NH):
            cast_pieces.append(("wo", _hh))
        for _e in range(NE):
            for _cob in range(NCOB):
                cast_pieces.append(("w1", _e, _cob))
            for _hh in range(NH):
                for _kcg in range(NKCG):
                    cast_pieces.append(("w2", _e, _hh, _kcg))
        cast_idx = [0]
        cast_rr = [0]

        def all_cast_toks():
            return tuple(D["gcw%d" % k].tok for k in range(4))

        def emit_cast_pieces(k):
            if cast_idx[0] >= len(cast_pieces):
                return
            @block.gpsimd
            def _(g):
                nonlocal tok_cast_kv
                for _ in range(k):
                    if cast_idx[0] >= len(cast_pieces):
                        return
                    pc = cast_pieces[cast_idx[0]]
                    cast_idx[0] += 1
                    wg.need(g, T.tok)
                    if pc[0] in ("kv", "wo"):
                        wg.need(g, D["gcast"].tok)
                        sem = D["gcast"]
                    else:
                        sem = D["gcw%d" % (cast_rr[0] % 4)]
                        cast_rr[0] += 1
                        wg.need(g, sem.tok)
                    if pc[0] == "kv":
                        j = pc[1]
                        D["gcast"].inc(g.dma_start(
                            bass.AP(wkc, j * P * KT * P,
                                    [[KT * P, P], [P, KT], [1, P]]),
                            bass.AP(wk, j * P, [[DM, P], [DM * P, KT], [1, P]])))
                        wg.need(g, D["gcast"].tok)
                        D["gcast"].inc(g.dma_start(
                            bass.AP(wvc, j * P * KT * P,
                                    [[KT * P, P], [P, KT], [1, P]]),
                            bass.AP(wv, j * P, [[DM, P], [DM * P, KT], [1, P]])))
                    elif pc[0] == "wo":
                        hh = pc[1]
                        tok = D["gcast"].inc(g.dma_start(
                            bass.AP(woc, hh * P * KT * MMW,
                                    [[KT * MMW, P], [MMW, KT], [1, MMW]]),
                            bass.AP(wo, hh * MMW,
                                    [[DM, P], [DM * P, KT], [1, MMW]])))
                        if hh == NH - 1:
                            tok_cast_kv = tok
                    elif pc[0] == "w1":
                        _e, cob = pc[1], pc[2]
                        sem.inc(g.dma_start(
                            bass.AP(w1c, (_e * NCOB + cob) * P * KT * W1CO,
                                    [[KT * W1CO, P], [W1CO, KT], [1, W1CO]]),
                            bass.AP(w1, _e * DM * DFF + cob * W1CO,
                                    [[DFF, P], [DFF * P, KT], [1, W1CO]])))
                        if cob == NCOB - 1:
                            tok_cast_w1[_e] = all_cast_toks()
                    else:
                        _e, hh, kcg = pc[1], pc[2], pc[3]
                        sem.inc(g.dma_start(
                            bass.AP(w2c,
                                    ((_e * NH + hh) * NKCG + kcg) * P * KCG * MMW,
                                    [[KCG * MMW, P], [MMW, KCG], [1, MMW]]),
                            bass.AP(w2, _e * DFF * DM + kcg * KCG * P * DM
                                    + hh * MMW,
                                    [[DM, P], [DM * P, KCG], [1, MMW]])))
                        if hh == NH - 1 and kcg == NKCG - 1:
                            tok_cast_w2[_e] = all_cast_toks()

        def finish_casts():
            emit_cast_pieces(len(cast_pieces))

        def force_casts_until(fn):
            while fn() is None and cast_idx[0] < len(cast_pieces):
                emit_cast_pieces(1)

        @block.vector
        def _(v):
            v = AE_V(v)
            v.memset(consts[:, 0:1], MAGIC)
            v.memset(consts[:, 1:2], PI_HALF)
            v.memset(consts[:, 2:3], 0.0)
            v.memset(consts[:, 3:4], 1e-5)
            v.memset(carries[:], 0.0)
            v.memset(bndc[:], 0.0)
            wvv.need(v, tok_setup_y)
            V.inc(v.tensor_scalar_mul(f2col[:], fcol[:], INV_2PI))
        tok_setup_v = V.tok

        with contextlib.ExitStack() as ph:
            psb = lambda nm, shp, dt=F32: ph.enter_context(nc.sbuf_tensor(nm, shp, dt))
            xbuf = psb("ln_x", [P, 4 * DM])
            xnf = [psb("ln_xn%d" % k_, [P, DM]) for k_ in range(2)]
            stats = [psb("ln_st%d" % k_, [P, 12]) for k_ in range(2)]
            mv = [psb("ln_mv%d" % k_, [P, 8]) for k_ in range(2)]
            xnf32b = [psb("ln_f32%d" % k_, [P, KT * P]) for k_ in range(2)]
            xnstg = psb("ln_stg", [P, 2 * KT * P], BF16)
            expb = psb("ln_exp", [P, NE])
            trps = [ph.enter_context(nc.psum_tensor("ln_tr%d" % k_, [P, KT * P], F32))
                    for k_ in range(2)]
            lps = ph.enter_context(nc.psum_tensor("ln_lg", [P, NE], F32))
            seps = ph.enter_context(nc.psum_tensor("ln_se", [NE, P], F32))

            NT2 = 2 * NLT if phases >= 1 else 0
            tok_x = [None] * NT2
            tok_v_stats = [None] * NT2
            tok_s_r = [None] * NT2
            tok_v_xn = [None] * NT2
            tok_t_tr = [None] * NT2
            tok_v_evac = [None] * NT2
            tok_s_stage = [None] * NT2
            tok_y_scr = [None] * NT2
            tok_t_rt = [None] * NT2
            tok_t_se = [None] * NT2
            tok_v_se = [None] * NT2
            tok_s_seem = [None] * NT2

            for i in range(NT2):
                fh = i < NLT
                ri = i if fh else i - NLT
                xsrc = xfh if fh else xown
                scol = (0 if fh else TOWN) + ri * P
                sl4 = i % 4
                pr = i % 2
                prev_own = i - 1 if i - 1 >= NLT else None
                prev_own2 = i - 2 if i - 2 >= NLT else None

                @block.sync
                def _(y, i=i, ri=ri, xsrc=xsrc, sl4=sl4):
                    if i >= 4:
                        wy.need(y, tok_v_xn[i - 4])
                    tok_x[i] = D["yx%d" % (sl4 % 2)].inc(y.dma_start(
                        xbuf[:, sl4 * DM:(sl4 + 1) * DM],
                        xsrc[ri * P:(ri + 1) * P, :]))

                @block.vector
                def _(v, i=i, sl4=sl4, pr=pr):
                    v = AE_V(v)
                    wvv.need(v, tok_x[i])
                    if i >= 2:
                        wvv.need(v, tok_s_r[i - 2])
                    v.bn_stats(stats[pr][:, 0:6], xbuf[:, sl4 * DM: sl4 * DM + DM // 2])
                    v.bn_stats(stats[pr][:, 6:12],
                               xbuf[:, sl4 * DM + DM // 2:(sl4 + 1) * DM])
                    v.bn_aggr(mv[pr][:, 0:2], stats[pr][:, 0:12])
                    v.tensor_scalar_add(mv[pr][:, 2:3], mv[pr][:, 1:2], consts[:, 3:4])
                    tok_v_stats[i] = V.tok

                @block.scalar
                def _(s, i=i, pr=pr):
                    s = AE_S(s)
                    ws.need(s, tok_v_stats[i])
                    s.activation(mv[pr][:, 3:4], mv[pr][:, 2:3], AF.Ln)
                    s.activation(mv[pr][:, 4:5], mv[pr][:, 3:4], AF.Exp, scale=-0.5)
                    tok_s_r[i] = S.tok

                @block.vector
                def _(v, i=i, sl4=sl4, pr=pr):
                    v = AE_V(v)
                    wvv.need(v, tok_s_r[i])
                    if i >= 2:
                        wvv.need(v, tok_t_tr[i - 2])
                    tok_v_xn[i] = V.inc(v.tensor_scalar(
                        xnf[pr][:], xbuf[:, sl4 * DM:(sl4 + 1) * DM],
                        mv[pr][:, 0:1], mv[pr][:, 4:5], ALU.subtract, ALU.mult))

                @block.tensor
                def _(t, i=i, pr=pr):
                    wt.need(t, tok_v_xn[i])
                    wt.need(t, tok_setup_y)
                    if i >= 2:
                        wt.need(t, tok_v_evac[i - 2])
                        wt.need(t, tok_s_stage[i - 2])
                    ins = None
                    for kt in range(KT):
                        ins = t.transpose(trps[pr][:, kt * P:(kt + 1) * P],
                                          xnf[pr][:, kt * P:(kt + 1) * P], identsb[:])
                    tok_t_tr[i] = T.inc(ins)

                @block.vector
                def _(v, i=i, sl=i % 2, pr=pr):
                    v = AE_V(v)
                    wvv.need(v, tok_t_tr[i])
                    if i >= 2:
                        wvv.need(v, tok_y_scr[i - 2])
                    for kt in range(KT):
                        v.tensor_copy(
                            xnstg[:, sl * KT * P + kt * P: sl * KT * P + (kt + 1) * P],
                            trps[pr][:, kt * P:(kt + 1) * P])
                    tok_s_stage[i] = V.tok

                @block.sync
                def _(y, i=i, scol=scol, sl=i % 2):
                    wy.need(y, tok_s_stage[i])
                    tok_y_scr[i] = D["yscr%d" % sl].inc(y.dma_start(
                        bass.AP(xnscr, scol, [[SCRW, P], [P * SCRW, KT], [1, P]]),
                        bass.AP(xnstg, sl * KT * P, [[2 * KT * P, P], [P, KT], [1, P]])))

                if i >= 2:
                    @block.gpsimd
                    def _(g):
                        wg.need(g, V.tok)
                    emit_cast_pieces(2)

                if not fh:
                    @block.vector
                    def _(v, i=i, pr=pr, prev_own2=prev_own2):
                        v = AE_V(v)
                        wvv.need(v, tok_t_tr[i])
                        if prev_own2 is not None:
                            wvv.need(v, tok_t_rt[prev_own2])
                        for kt in range(KT):
                            v.tensor_copy(xnf32b[pr][:, kt * P:(kt + 1) * P],
                                          trps[pr][:, kt * P:(kt + 1) * P])
                        tok_v_evac[i] = V.tok

                    @block.tensor
                    def _(t, i=i, pr=pr, prev_own=prev_own):
                        wt.need(t, tok_v_evac[i])
                        if prev_own is not None:
                            wt.need(t, tok_v_se[prev_own])
                        ins = None
                        for kt in range(KT):
                            ins = t.matmul(lps[:, 0:NE],
                                           xnf32b[pr][:, kt * P:(kt + 1) * P],
                                           gwsb[:, kt * NE:(kt + 1) * NE],
                                           start=(kt == 0), stop=(kt == KT - 1))
                        tok_t_rt[i] = T.inc(ins)

                    @block.vector
                    def _(v, i=i, pr=pr):
                        v = AE_V(v)
                        wvv.need(v, tok_t_rt[i])
                        v.reduce_max(mv[pr][:, 5:6], lps[:, 0:NE], axis=AX.X)
                        V.inc(v.reduce_max(mv[pr][:, 6:7], lps[:, 0:NE], axis=AX.X,
                                           negate=True))

                    @block.scalar
                    def _(s, i=i, pr=pr):
                        s = AE_S(s)
                        ws.need(s, V.tok)
                        S.inc(s.activation(expb[:], lps[:, 0:NE], AF.Exp,
                                           bias=mv[pr][:, 6:7]))

                    @block.vector
                    def _(v, i=i, ri=ri, pr=pr):
                        v = AE_V(v)
                        wvv.need(v, S.tok)
                        v.reduce_sum(mv[pr][:, 7:8], expb[:], axis=AX.X)
                        v.reciprocal(mv[pr][:, 7:8], mv[pr][:, 7:8])
                        tok_v_se[i] = V.inc(v.tensor_scalar(
                            se_tm[:, ri * NE:(ri + 1) * NE], lps[:, 0:NE],
                            mv[pr][:, 5:6], mv[pr][:, 7:8], ALU.is_equal, ALU.mult))

                    @block.tensor
                    def _(t, i=i, ri=ri, prev_own=prev_own):
                        wt.need(t, tok_v_se[i])
                        if prev_own is not None:
                            wt.need(t, tok_s_seem[prev_own])
                        tok_t_se[i] = T.inc(t.transpose(
                            seps[:, 0:P], se_tm[:, ri * NE:(ri + 1) * NE], identsb[:]))

                    @block.vector
                    def _(v, i=i, ri=ri):
                        v = AE_V(v)
                        wvv.need(v, tok_t_se[i])
                        v.tensor_copy(se_em[:, ri * P:(ri + 1) * P], seps[:, 0:P])
                        tok_s_seem[i] = V.tok

            @block.vector
            def _(v):
                v = AE_V(v)
                wvv.need(v, S.tok)
                wvv.need(v, T.tok)
                V.inc(v.tensor_copy(consts[:, 1:2], consts[:, 1:2]))
            fence1 = V.tok
            for blk in (block.tensor, block.scalar, block.sync, block.gpsimd):
                @blk
                def _(e):
                    e.wait_ge(sem_v, fence1[1])
            for w_ in (wt, ws, wy, wg):
                w_.bump(fence1)
            tok_scr_hold = ((tok_y_scr[NT2 - 2], tok_y_scr[NT2 - 1])
                            if NT2 >= 2 else (None, None))

        with contextlib.ExitStack() as ph:
            psb = lambda nm, shp, dt=F32: ph.enter_context(nc.sbuf_tensor(nm, shp, dt))
            xna = psb("a_xn", [P, KT * ACH], BF16)
            t_rep = psb("a_t", [P, ACH])
            rs_rep = psb("a_rs", [P, ACH], BF16)
            bphi = psb("a_bph", [P, ACH])
            bnphi = psb("a_bnp", [P, ACH])
            bthe = psb("a_bth", [P, ACH])
            bnthe = psb("a_bnt", [P, ACH])
            costh = psb("a_cth", [P, ACH])
            sinth = psb("a_sth", [P, ACH])
            cosph = psb("a_cph", [P, ACH + 1])
            sinph = psb("a_sph", [P, ACH + 1])
            pbuf = psb("a_p", [P, ACH])
            vsb = psb("a_vs", [P, 2 * ACH], BF16)
            cbuf = psb("a_cb", [P, ACH])
            tmpA = psb("a_tA", [P, ACH])
            tmpB = psb("a_tB", [P, ACH])
            zfm = psb("a_z", [P, KT * ACH], BF16)
            wkb = psb("a_wk", [P, KT * P], BF16)
            wvb = psb("a_wv", [P, KT * P], BF16)
            wob = psb("a_wo", [P, KT * MMW], BF16)
            xnf = psb("f_xn", [P, 2 * KT * FCH], BF16)
            w2ring = psb("f_w2", [P, W2SLOTS * KCG * MMW], BF16)
            hsb = psb("f_h", [P, NCO * FCH], BF16)
            hps = [ph.enter_context(nc.psum_tensor("f_h%d" % k_, [P, FCH], F32))
                   for k_ in range(NHPS)]
            yps = [ph.enter_context(nc.psum_tensor("f_y%d" % k_, [P, MMW], F32))
                   for k_ in range(NTT)]
            kvp = [ph.enter_context(nc.psum_tensor("a_kv%d" % k_, [P, ACH], F32))
                   for k_ in range(2)]

            zcol_ap = lambda n: bass.AP(consts, 2, [[4, P], [0, n]])

            chunks = ([("fh", c) for c in range(NACH)] if phases >= 2 else []) + \
                     ([("own", c) for c in range(NACH)] if phases >= 3 else [])
            tok_y_wk = {}
            tok_y_wv = {}
            tok_t_kv = {}
            tok_s_A = {}
            tok_s_B = {}
            tok_v_rr = {}
            tok_v_p4 = {}
            tok_v_z = {}
            tok_y_xn = [None] * len(chunks)
            tok_y_trep = [None] * len(chunks)
            tok_t_wo = {}
            tok_v_aacc = {}
            tok_g_wo = [None] * len(chunks)
            tok_v_acc = {}

            def au_chunk_head(ci):
                force_casts_until(lambda: tok_cast_kv)
                kind, c = chunks[ci]
                own = kind == "own"
                scol0 = (TOWN if own else 0) + c * ACH

                @block.sync
                def _(y, ci=ci, own=own, c=c, scol0=scol0):
                    if ci == 0:
                        wy.need(y, tok_scr_hold[0])
                        wy.need(y, tok_scr_hold[1])
                    if ci > 0:
                        wy.need(y, tok_t_kv[(ci - 1, KT - 1)])
                    tok_y_xn[ci] = D["yxna"].inc(y.dma_start(
                        bass.AP(xna, 0, [[KT * ACH, P], [ACH, KT], [1, ACH]]),
                        bass.AP(xnscr, scol0, [[SCRW, P], [P * SCRW, KT], [1, ACH]])))
                    if ci > 0:
                        wy.need(y, tok_s_A[(ci - 1, KT - 1)])
                        wy.need(y, tok_v_z[(ci - 1, KT - 1)])
                    trow = trow_own if own else trow_fh
                    D["ytr"].inc(y.dma_start(
                        t_rep[:], bass.AP(trow, c * ACH, [[0, P], [1, ACH]])))
                    tok_y_trep[ci] = D["ytr"].inc(y.dma_start(
                        rs_rep[:], bass.AP(rsrow, c * ACH, [[0, P], [1, ACH]])))


            def au_chtile(ci, j):
                kind, c = chunks[ci]
                own = kind == "own"
                gj = ci * KT + j

                @block.sync
                def _(y, ci=ci, j=j, gj=gj):
                    wy.need(y, tok_cast_kv)
                    if gj >= 1:
                        pci, pjj = (gj - 1) // KT, (gj - 1) % KT
                        wy.need(y, tok_t_kv[(pci, pjj)])
                    tok_y_wk[(ci, j)] = D["ywk0"].inc(y.dma_start(
                        bass.AP(wkb, 0, [[KT * P, P], [1, KT * P]]),
                        bass.AP(wkc, j * P * KT * P, [[KT * P, P], [1, KT * P]])))
                    tok_y_wv[(ci, j)] = D["ywv0"].inc(y.dma_start(
                        bass.AP(wvb, 0, [[KT * P, P], [1, KT * P]]),
                        bass.AP(wvc, j * P * KT * P, [[KT * P, P], [1, KT * P]])))

                @block.tensor
                def _(t, ci=ci, j=j, gj=gj):
                    wt.need(t, tok_y_wk[(ci, j)])
                    wt.need(t, tok_y_wv[(ci, j)])
                    wt.need(t, tok_y_xn[ci])
                    if gj >= 1:
                        pci, pjj = (gj - 1) // KT, (gj - 1) % KT
                        wt.need(t, tok_s_A[(pci, pjj)])
                    if j == 0:
                        for hh2 in range(NH):
                            if (ci - 1, hh2, ATT - 1) in tok_v_aacc:
                                wt.need(t, tok_v_aacc[(ci - 1, hh2, ATT - 1)])
                    ins = None
                    for kt in range(KT):
                        t.matmul(kvp[0][:, 0:ACH],
                                 wkb[:, kt * P:(kt + 1) * P],
                                 xna[:, kt * ACH:(kt + 1) * ACH],
                                 start=(kt == 0), stop=(kt == KT - 1))
                    for kt in range(KT):
                        ins = t.matmul(kvp[1][:, 0:ACH],
                                       wvb[:, kt * P:(kt + 1) * P],
                                       xna[:, kt * ACH:(kt + 1) * ACH],
                                       start=(kt == 0), stop=(kt == KT - 1))
                    tok_t_kv[(ci, j)] = T.inc(ins)

                @block.scalar
                def _(s, ci=ci, j=j, gj=gj):
                    s = AE_S(s)
                    ws.need(s, tok_t_kv[(ci, j)])
                    ws.need(s, tok_y_trep[ci])
                    if gj >= 1:
                        pci, pjj = (gj - 1) // KT, (gj - 1) % KT
                        ws.need(s, tok_v_rr[(pci, pjj)])
                    if gj >= 2:
                        pci, pjj = (gj - 2) // KT, (gj - 2) % KT
                        ws.need(s, tok_v_p4[(pci, pjj)])
                    s.activation(vsb[:, (gj % 2) * ACH:(gj % 2 + 1) * ACH],
                                 kvp[1][:, 0:ACH], AF.Identity, scale=1.0)
                    s.activation(bphi[:], kvp[0][:, 0:ACH], AF.Identity, scale=10.0)
                    s.activation(bnphi[:], kvp[0][:, 0:ACH], AF.Identity,
                                 bias=consts[:, 0:1], scale=10.0 * INV_2PI)
                    s.activation(bthe[:], t_rep[:], AF.Identity, scale=fcol[:, j:j + 1])
                    s.activation(bnthe[:], t_rep[:], AF.Identity,
                                 bias=consts[:, 0:1], scale=f2col[:, j:j + 1])
                    tok_s_A[(ci, j)] = S.tok

                @block.vector
                def _(v, ci=ci, j=j, gj=gj):
                    v = AE_V(v)
                    wvv.need(v, tok_s_A[(ci, j)])
                    v.tensor_scalar(tmpA[:], bnphi[:], -MAGIC, -TWO_PI,
                                    ALU.add, ALU.mult)
                    v.tensor_tensor(bphi[:], bphi[:], tmpA[:], ALU.add)
                    v.tensor_scalar(bphi[:], bphi[:], PI_LO, -PI_LO, ALU.min, ALU.max)
                    v.tensor_scalar(tmpA[:], bnthe[:], -MAGIC, -TWO_PI,
                                    ALU.add, ALU.mult)
                    v.tensor_tensor(bthe[:], bthe[:], tmpA[:], ALU.add)
                    v.tensor_scalar(bthe[:], bthe[:], PI_LO, -PI_LO, ALU.min, ALU.max)
                    tok_v_rr[(ci, j)] = V.tok

                @block.scalar
                def _(s, ci=ci, j=j, gj=gj):
                    s = AE_S(s)
                    ws.need(s, tok_v_rr[(ci, j)])
                    if gj >= 1:
                        pci, pjj = (gj - 1) // KT, (gj - 1) % KT
                        ws.need(s, tok_v_z[(pci, pjj)])
                    s.activation(sinph[:, 1:ACH + 1], bphi[:], AF.Sin)
                    s.activation(bnphi[:], bphi[:], AF.Abs)
                    s.activation(cosph[:, 1:ACH + 1], bnphi[:], AF.Sin,
                                 bias=consts[:, 1:2], scale=-1.0)
                    s.activation(sinth[:], bthe[:], AF.Sin)
                    s.activation(bnthe[:], bthe[:], AF.Abs)
                    s.activation(costh[:], bnthe[:], AF.Sin,
                                 bias=consts[:, 1:2], scale=-1.0)
                    tok_s_B[(ci, j)] = S.tok

                @block.vector
                def _(v, ci=ci, j=j, own=own, c=c):
                    v = AE_V(v)
                    wvv.need(v, tok_s_B[(ci, j)])
                    wvv.need(v, tok_setup_v)
                    v.tensor_copy(cosph[:, 0:1], bndc[:, 2 * j:2 * j + 1])
                    v.tensor_copy(sinph[:, 0:1], bndc[:, 2 * j + 1:2 * j + 2])
                    v.tensor_copy(bndc[:, 2 * j:2 * j + 1], cosph[:, ACH:ACH + 1])
                    v.tensor_copy(bndc[:, 2 * j + 1:2 * j + 2], sinph[:, ACH:ACH + 1])
                    binders = [costh[:, 0:ACH], sinth[:, 0:ACH],
                               cosph[:, 0:ACH], sinph[:, 0:ACH]]
                    probes = [costh[:, 0:ACH], sinth[:, 0:ACH],
                              cosph[:, 1:ACH + 1], sinph[:, 1:ACH + 1]]
                    if own and (ci - 1, 1, ATT - 1) in tok_t_wo:
                        wvv.need(v, tok_t_wo[(ci - 1, 1, ATT - 1)])
                    gj = ci * KT + j
                    for sn in range(4):
                        v.tensor_tensor(pbuf[:],
                                        vsb[:, (gj % 2) * ACH:(gj % 2 + 1) * ACH],
                                        binders[sn], ALU.mult)
                        if sn == 3:
                            tok_v_p4[(ci, j)] = V.tok
                        v.tensor_tensor_scan(
                            cbuf[:], pbuf[:], zcol_ap(ACH),
                            carries[:, 4 * j + sn:4 * j + sn + 1],
                            ALU.add, ALU.add)
                        v.tensor_copy(carries[:, 4 * j + sn:4 * j + sn + 1],
                                      cbuf[:, ACH - 1:ACH])
                        if own:
                            if sn == 0:
                                v.tensor_tensor(tmpA[:], cbuf[:], probes[0], ALU.mult)
                            elif sn == 1:
                                v.tensor_tensor(tmpB[:], cbuf[:], probes[1], ALU.mult)
                                v.tensor_tensor(tmpA[:], tmpA[:], tmpB[:], ALU.add)
                            elif sn == 2:
                                v.tensor_tensor(tmpB[:], cbuf[:], probes[2], ALU.mult)
                            else:
                                v.tensor_tensor(pbuf[:], cbuf[:], probes[3], ALU.mult)
                    if own:
                        v.tensor_tensor(tmpB[:], tmpB[:], pbuf[:], ALU.add)
                        v.tensor_scalar_mul(tmpA[:], tmpA[:], gposc[:, j:j + 1])
                        v.scalar_tensor_tensor(tmpA[:], tmpB[:], gac[:, j:j + 1],
                                               tmpA[:], ALU.mult, ALU.add)
                        wvv.need(v, tok_y_trep[ci])
                        v.tensor_tensor(zfm[:, j * ACH:(j + 1) * ACH], tmpA[:],
                                        rs_rep[:], ALU.mult)
                    tok_v_z[(ci, j)] = V.tok

            def au_bmask():
                @block.vector
                def _(v):
                    v = AE_V(v)
                    v.tensor_scalar_mul(carries[:], carries[:], bmaskc[:, 0:1])
                    v.tensor_scalar_mul(bndc[:], bndc[:], bmaskc[:, 0:1])

            def au_wo(ci, hh):
                kind, c = chunks[ci]

                @block.sync
                def _(y, ci=ci, hh=hh):
                    wy.need(y, tok_cast_kv)
                    if hh == 0 and (ci - 1, 1, ATT - 1) in tok_t_wo:
                        wy.need(y, tok_t_wo[(ci - 1, 1, ATT - 1)])
                    if hh == 1:
                        wy.need(y, tok_t_wo[(ci, 0, ATT - 1)])
                    tok_g_wo[ci] = D["ywo"].inc(y.dma_start(
                        bass.AP(wob, 0, [[KT * MMW, P], [1, KT * MMW]]),
                        bass.AP(woc, hh * P * KT * MMW,
                                [[KT * MMW, P], [1, KT * MMW]])))

                for tt in range(ATT):
                    ttg = c * ATT + tt

                    @block.tensor
                    def _(t, ci=ci, c=c, tt=tt, ttg=ttg, hh=hh):
                        for jj in range(KT):
                            wt.need(t, tok_v_z[(ci, jj)])
                        wt.need(t, tok_g_wo[ci])
                        wt.need(t, tok_b2)
                        wt.need(t, tok_s_A[(ci, KT - 1)])
                        if (ci, hh, tt - 1) in tok_v_aacc:
                            wt.need(t, tok_v_aacc[(ci, hh, tt - 1)])
                        ins = None
                        for jj in range(KT):
                            t.matmul(kvp[hh][:, 0:MMW],
                                     zfm[:, jj * ACH + tt * P: jj * ACH + (tt + 1) * P],
                                     wob[:, jj * MMW:(jj + 1) * MMW],
                                     start=(jj == 0), stop=False)
                        ins = t.matmul(
                            kvp[hh][:, 0:MMW],
                            bass.AP(se_em, c * ACH + tt * P, [[TOWN, NE], [1, P]]),
                            bass.AP(b2sb, hh * MMW, [[DM, NE], [1, MMW]]),
                            start=False, stop=True)
                        tok_t_wo[(ci, hh, tt)] = T.inc(ins)

                    @block.vector
                    def _(v, ci=ci, c=c, tt=tt, ttg=ttg, hh=hh):
                        v = AE_V(v)
                        wvv.need(v, tok_t_wo[(ci, hh, tt)])
                        if (0, c, tt) in tok_v_acc:
                            wvv.need(v, tok_v_acc[(0, c, tt)])
                        v.scalar_tensor_tensor(
                            accf[:, ttg * DM + hh * MMW: ttg * DM + (hh + 1) * MMW],
                            kvp[hh][:, 0:MMW], 1.0,
                            accf[:, ttg * DM + hh * MMW: ttg * DM + (hh + 1) * MMW],
                            ALU.mult, ALU.add)
                        tok_v_aacc[(ci, hh, tt)] = V.tok

            attn_units = []
            for ci in range(len(chunks)):
                kind, c = chunks[ci]
                attn_units.append(lambda ci=ci: au_chunk_head(ci))
                for j in range(KT):
                    attn_units.append(lambda ci=ci, j=j: au_chtile(ci, j))
                if kind == "fh" and ci == NACH - 1:
                    attn_units.append(au_bmask)
                if kind == "own":
                    for hh in range(NH):
                        attn_units.append(lambda ci=ci, hh=hh: au_wo(ci, hh))
            au_idx = [0]

            def emit_attn_units(upto):
                while au_idx[0] < min(upto, len(attn_units)):
                    attn_units[au_idx[0]]()
                    au_idx[0] += 1

            tok_g_w1 = {}
            tok_g_w2p = {}
            tok_t_f1 = {}
            tok_t_f1ct = {}
            tok_s_h = {}
            tok_t_w2free = {}
            tok_t_f2stop = {}
            tok_t_hsbfree = {}
            tok_v_yev = {}
            tok_y_fxn = {}
            NCH = (NE * NFCH) if phases >= 4 else 0
            gpc = [0]

            def emit_w2_pieces(e, hh, up0):
                force_casts_until(lambda: tok_cast_w2[e])
                for kcg in range(NKCG):
                    up = up0 + kcg
                    slot = up % W2SLOTS

                    @block.gpsimd
                    def _(g, e=e, hh=hh, kcg=kcg, up=up, slot=slot):
                        if up >= W2SLOTS:
                            wg.need(g, tok_t_w2free[up - W2SLOTS])
                        wg.need(g, tok_cast_w2[e])
                        tok_g_w2p[up] = D["gp%d" % slot].inc(g.dma_start(
                            bass.AP(w2ring, slot * KCG * MMW,
                                    [[W2SLOTS * KCG * MMW, P], [1, KCG * MMW]]),
                            bass.AP(w2c,
                                    ((e * NH + hh) * NKCG + kcg) * P * KCG * MMW,
                                    [[KCG * MMW, P], [1, KCG * MMW]])))

            SLOTS_TOTAL = max(1, NCH * (NCOB + NH))
            au_slot = [0]

            def au_tick():
                au_slot[0] += 1
                emit_attn_units(au_slot[0] * len(attn_units) // SLOTS_TOTAL)
                if au_slot[0] % 4 == 0:
                    emit_cast_pieces(1)

            for gi in range(NCH):
                e, c = gi // NFCH, gi % NFCH
                sl = c % 2
                gcob0 = gi * NCOB

                emit_w2_pieces(e, 0, gpc[0])

                @block.sync
                def _(y, e=e, c=c, sl=sl, gi=gi):
                    wy.need(y, tok_scr_hold[0])
                    wy.need(y, tok_scr_hold[1])
                    pgi = gi - 2
                    if pgi >= 0:
                        pk = (pgi // NFCH, pgi % NFCH)
                        if (pk[0], pk[1], NCOB - 1) in tok_t_f1:
                            wy.need(y, tok_t_f1[(pk[0], pk[1], NCOB - 1)])
                    tok_y_fxn[(e, c)] = D["yxnf%d" % sl].inc(y.dma_start(
                        bass.AP(xnf, sl * KT * FCH,
                                [[2 * KT * FCH, P], [FCH, KT], [1, FCH]]),
                        bass.AP(xnscr, TOWN + c * FCH,
                                [[SCRW, P], [P * SCRW, KT], [1, FCH]])))

                force_casts_until(lambda: tok_cast_w1[e])
                for cob in range(NCOB):
                    slot = (gcob0 + cob) % RING

                    @block.sync
                    def _(y, e=e, c=c, cob=cob, slot=slot, gcob0=gcob0):
                        prev = gcob0 + cob - RING
                        if prev >= 0:
                            pe = prev // (NFCH * NCOB)
                            pc = (prev // NCOB) % NFCH
                            pcob = prev % NCOB
                            wy.need(y, tok_t_f1[(pe, pc, pcob)])
                        wy.need(y, tok_cast_w1[e])
                        tok_g_w1[gcob0 + cob] = D["yw1%d" % slot].inc(y.dma_start(
                            bass.AP(w1ring, slot * KT * W1CO,
                                    [[RING * KT * W1CO, P], [1, KT * W1CO]]),
                            bass.AP(w1c, (e * NCOB + cob) * P * KT * W1CO,
                                    [[KT * W1CO, P], [1, KT * W1CO]])))

                    def emit_f1_mms(ctb, e=e, c=c, cob=cob, slot=slot, sl=sl,
                                    gcob0=gcob0):
                        @block.tensor
                        def _(t, ctb=ctb):
                            wt.need(t, tok_g_w1[gcob0 + cob])
                            wt.need(t, tok_y_fxn[(e, c)])
                            for ct in range(ctb, min(ctb + NHPS, NW1C)):
                                ctg = cob * NW1C + ct
                                pr = ctg - NHPS
                                if pr >= 0:
                                    wt.need(t, tok_s_h[(e, c, pr)])
                                else:
                                    pk = (e, c - 1) if c > 0 else (e - 1, NFCH - 1)
                                    if pk[0] >= 0:
                                        wt.need(t, tok_s_h[(pk[0], pk[1],
                                                            NCO + pr)])
                                ins = None
                                for kt in range(KT):
                                    ins = t.matmul(
                                        hps[ctg % NHPS][:, 0:FCH],
                                        w1ring[:, slot * KT * W1CO + kt * W1CO
                                               + ct * P:
                                               slot * KT * W1CO + kt * W1CO
                                               + (ct + 1) * P],
                                        xnf[:, sl * KT * FCH + kt * FCH:
                                            sl * KT * FCH + (kt + 1) * FCH],
                                        start=(kt == 0), stop=(kt == KT - 1))
                                tok_t_f1ct[(e, c, ctg)] = T.inc(ins)

                    def emit_f1_gelus(e=e, c=c, cob=cob, gi=gi):
                        base = T.n

                        @block.scalar
                        def _(s):
                            s = AE_S(s)
                            ws.need(s, tok_setup_y)
                            pgi = gi - 1
                            if pgi >= 0:
                                for grp in sorted({(cob * NW1C + ct) // KCG
                                                   for ct in range(NW1C)}):
                                    ws.need(s, tok_t_hsbfree[(pgi, grp)])
                            for ct in range(NW1C):
                                ctg = cob * NW1C + ct
                                if (e, c, ctg) in tok_t_f1ct:
                                    ws.need(s, tok_t_f1ct[(e, c, ctg)])
                                else:
                                    ws.need(s, (T, base + ct - NHPS + 1))
                                ins = s.activation(
                                    hsb[:, ctg * FCH:(ctg + 1) * FCH],
                                    hps[ctg % NHPS][:, 0:FCH], AF.Gelu,
                                    bias=b1sb[:, e * NCO + ctg:
                                              e * NCO + ctg + 1])
                                tok_s_h[(e, c, ctg)] = S.inc(ins)

                    emit_f1_mms(0)
                    emit_f1_gelus()
                    for ctb in range(NHPS, NW1C, NHPS):
                        emit_f1_mms(ctb)
                    tok_t_f1[(e, c, cob)] = (T, T.n)
                    au_tick()

                for hh in range(NH):
                    up0 = gpc[0]
                    uh = gi * NH + hh

                    @block.tensor
                    def _(t, e=e, c=c, hh=hh, gi=gi, up0=up0, uh=uh):
                        for kcg in range(NKCG):
                            up = up0 + kcg
                            slot = up % W2SLOTS
                            wt.need(t, tok_g_w2p[up])
                            wt.need(t, tok_s_h[(e, c, kcg * KCG + KCG - 1)])
                            if kcg == 0 and uh >= 1:
                                for tt in range(NTT):
                                    wt.need(t, tok_v_yev[(uh - 1, tt)])
                            ins = None
                            for tt in range(NTT):
                                for k2 in range(KCG):
                                    kc = kcg * KCG + k2
                                    ins = t.matmul(
                                        yps[tt][:, 0:MMW],
                                        hsb[:, kc * FCH + tt * P:
                                            kc * FCH + (tt + 1) * P],
                                        w2ring[:, slot * KCG * MMW + k2 * MMW:
                                               slot * KCG * MMW
                                               + (k2 + 1) * MMW],
                                        start=(kcg == 0 and k2 == 0),
                                        stop=(kcg == NKCG - 1
                                              and k2 == KCG - 1))
                                if kcg == NKCG - 1:
                                    tok_t_f2stop[(uh, tt)] = T.inc(ins)
                            tok_t_w2free[up] = T.inc(ins)
                            if hh == NH - 1:
                                tok_t_hsbfree[(gi, kcg)] = tok_t_w2free[up]
                    gpc[0] += NKCG
                    if hh + 1 < NH:
                        emit_w2_pieces(e, hh + 1, gpc[0])
                    elif gi + 1 < NCH:
                        pass

                    for tt in range(NTT):
                        ttg = c * NTT + tt

                        @block.vector
                        def _(v, e=e, c=c, tt=tt, ttg=ttg, hh=hh, uh=uh):
                            v = AE_V(v)
                            wvv.need(v, tok_t_f2stop[(uh, tt)])
                            dst = accf[:, ttg * DM + hh * MMW:
                                       ttg * DM + (hh + 1) * MMW]
                            if e == 0:
                                v.tensor_scalar(
                                    dst, yps[tt][:, 0:MMW],
                                    se_tm[:, ttg * NE + e:ttg * NE + e + 1],
                                    None, ALU.mult)
                            else:
                                v.scalar_tensor_tensor(
                                    dst, yps[tt][:, 0:MMW],
                                    se_tm[:, ttg * NE + e:ttg * NE + e + 1],
                                    dst, ALU.mult, ALU.add)
                            tok_v_yev[(uh, tt)] = V.tok
                            if e == 0:
                                tok_v_acc[(0, c, tt)] = V.tok
                    au_tick()

            emit_attn_units(len(attn_units))
            finish_casts()

            @block.vector
            def _(v):
                v = AE_V(v)
                wvv.need(v, T.tok)
                wvv.need(v, S.tok)
                V.inc(v.tensor_copy(consts[:, 1:2], consts[:, 1:2]))
            fence3 = V.tok
            for blk in (block.tensor, block.scalar, block.sync, block.gpsimd):
                @blk
                def _(e):
                    e.wait_ge(sem_v, fence3[1])
            for w_ in (wt, ws, wy, wg):
                w_.bump(fence3)

        with contextlib.ExitStack() as ph:
            psb = lambda nm, shp, dt=F32: ph.enter_context(nc.sbuf_tensor(nm, shp, dt))
            xb2 = psb("o_x", [P, 2 * DM])
            ob = psb("o_o", [P, 2 * DM])
            tok_y_x2 = [None] * NLT
            tok_v_o = [None] * NLT
            tok_y_out = [None] * NLT

            for i in range(NLT):
                sl = i % 2

                @block.sync
                def _(y, i=i, sl=sl):
                    if i >= 2:
                        wy.need(y, tok_v_o[i - 2])
                    tok_y_x2[i] = D["yx%d" % sl].inc(y.dma_start(
                        xb2[:, sl * DM:(sl + 1) * DM], xown[i * P:(i + 1) * P, :]))

                @block.vector
                def _(v, i=i, sl=sl):
                    v = AE_V(v)
                    wvv.need(v, tok_y_x2[i])
                    if i >= 2:
                        wvv.need(v, tok_y_out[i - 2])
                    tok_v_o[i] = V.inc(v.tensor_tensor(
                        ob[:, sl * DM:(sl + 1) * DM],
                        xb2[:, sl * DM:(sl + 1) * DM],
                        accf[:, i * DM:(i + 1) * DM], ALU.add))

                @block.sync
                def _(y, i=i, sl=sl):
                    wy.need(y, tok_v_o[i])
                    tok_y_out[i] = D["yout%d" % sl].inc(y.dma_start(
                        out_d[i * P:(i + 1) * P, :], ob[:, sl * DM:(sl + 1) * DM]))

            @block.sync
            def _(y):
                y.wait_ge(D["yout0"].sem, D["yout0"].n)
                y.wait_ge(D["yout1"].sem, D["yout1"].n)
            @block.gpsimd
            def _(g):
                g.wait_ge(D["yout0"].sem, D["yout0"].n)
                g.wait_ge(D["yout1"].sem, D["yout1"].n)

    return nc


_NC_CACHE = {}


def _get_nc():
    if "full" not in _NC_CACHE:
        _NC_CACHE["full"] = build_program(Cfg())
    return _NC_CACHE["full"]


def host_inputs(cfg, core, x_full, Wk, Wv, Wo, gate, freqs, gate_w, w1, b1, w2, b2):
    b, half = core // 2, core % 2
    TOWN = cfg.TOWN
    T_batch = 2 * TOWN
    Dh = freqs.shape[1]
    f32 = np.float32
    own0 = half * TOWN
    fflat = np.ascontiguousarray(freqs.reshape(-1))
    gpos = np.repeat(gate[:, 0], Dh)
    gassoc = np.repeat(gate[:, 1], Dh)
    col = lambda vec: np.ascontiguousarray(vec.reshape(cfg.KT, P).T).astype(f32)
    b1t = np.ascontiguousarray(
        b1.reshape(cfg.NE, cfg.NCO, P).transpose(2, 0, 1).reshape(P, cfg.NE * cfg.NCO))
    tg = np.arange(T_batch, dtype=f32)
    return {
        "xown": np.ascontiguousarray(x_full[b, own0:own0 + TOWN]),
        "xfh": np.ascontiguousarray(x_full[b, 0:TOWN]),
        "wk": Wk, "wv": Wv, "wo": Wo, "gate_w": gate_w,
        "w1": w1, "w2": w2, "b2": b2, "b1t": b1t,
        "fcol": col(fflat), "gpos": col(gpos), "gassoc": col(gassoc),
        "trow_own": np.ascontiguousarray(tg[None, own0:own0 + TOWN]),
        "trow_fh": np.ascontiguousarray(tg[None, 0:TOWN]),
        "rsrow": np.ascontiguousarray(
            (1.0 / np.sqrt(tg[own0:own0 + TOWN] + 1.0))[None, :]).astype(
                ml_dtypes.bfloat16),
        "ident": np.eye(P, dtype=f32),
        "bmask": np.full((P, 1), float(half), dtype=f32),
    }


def kernel(L0_attn_x, L0_attn_Wk, L0_attn_Wv, L0_attn_Wo, L0_attn_gate,
           L0_attn_freqs, L0_ffn_gate_w, L0_ffn_w1, L0_ffn_b1, L0_ffn_w2,
           L0_ffn_b2):
    cfg = Cfg()
    nc = _get_nc()
    f32 = np.float32
    x = np.asarray(L0_attn_x, f32)
    args = (np.asarray(L0_attn_Wk, f32), np.asarray(L0_attn_Wv, f32),
            np.asarray(L0_attn_Wo, f32), np.asarray(L0_attn_gate, f32),
            np.asarray(L0_attn_freqs, f32), np.asarray(L0_ffn_gate_w, f32),
            np.asarray(L0_ffn_w1, f32), np.asarray(L0_ffn_b1, f32),
            np.asarray(L0_ffn_w2, f32), np.asarray(L0_ffn_b2, f32))
    in_maps = [host_inputs(cfg, c, x, *args) for c in range(8)]
    res = run_bass_kernel_spmd(nc, in_maps, core_ids=list(range(8)))
    B, Tt, DMm = x.shape
    out = np.empty((B, Tt, DMm), f32)
    for c in range(8):
        b, half = c // 2, c % 2
        out[b, half * cfg.TOWN:(half + 1) * cfg.TOWN] = res.results[c]["out"]
    return out
